# version 30
# baseline (speedup 1.0000x reference)
import contextlib
import numpy as np
import ml_dtypes
import concourse.bass as bass
import concourse.mybir as mybir
from concourse.bass_utils import run_bass_kernel_spmd

F32 = mybir.dt.float32
BF16 = mybir.dt.bfloat16
I32 = mybir.dt.int32
AF = mybir.ActivationFunctionType
ALU = mybir.AluOpType
AX = mybir.AxisListType
bf16 = ml_dtypes.bfloat16

D = 4096
TL = 1024
SEQ = 4096
NCH = 32
NHL = 4
SC = 0.0625
RG = [[0, 1, 2, 3], [4, 5, 6, 7]]

ENGS = ("sync", "gpsimd", "vector", "scalar", "tensor")


class Plan:

    def __init__(self, mksem):
        self.q = {e: [] for e in ENGS}
        self.ctr = {}
        self.sem = {}
        self.mksem = mksem

    def _get(self, name):
        if name not in self.sem:
            self.sem[name] = self.mksem(name)
            self.ctr[name] = 0
        return self.sem[name]

    def now(self, s):
        self._get(s)
        return self.ctr[s]

    def op(self, eng, mk, sem=None, n=1):
        if sem is None:
            self.q[eng].append(mk)
            return None
        h = self._get(sem)
        self.q[eng].append(lambda e, mk=mk, h=h, n=n: mk(e).then_inc(h, n))
        self.ctr[sem] += n
        return self.ctr[sem]

    def wait(self, eng, sem, val):
        if val is None or val <= 0:
            return
        h = self._get(sem)
        assert val <= self.ctr[sem], f"wait {sem}>={val} but only {self.ctr[sem]} planned"
        self.q[eng].append(lambda e, h=h, v=val: e.wait_ge(h, v))

    def dma(self, eng, out, in_, slot):
        return self.op(eng, lambda e, o=out, i=in_: e.dma_start(out=o, in_=i),
                       "D" + slot, 16)

    def emit(self, nc):
        with nc.Block() as block:
            @block.sync
            def _(e):
                for f in self.q["sync"]:
                    f(e)

            @block.gpsimd
            def _(e):
                for f in self.q["gpsimd"]:
                    f(e)

            @block.vector
            def _(e):
                for f in self.q["vector"]:
                    f(e)

            @block.scalar
            def _(e):
                for f in self.q["scalar"]:
                    f(e)

            @block.tensor
            def _(e):
                for f in self.q["tensor"]:
                    f(e)


def build_program(taps=()):
    nc = bass.Bass("TRN2", target_bir_lowering=False, debug=False)

    def din(name, shape, dt):
        return nc.dram_tensor(name, shape, dt, kind="ExternalInput").ap()

    def dint(name, shape, dt):
        return nc.dram_tensor(name, shape, dt, kind="Internal").ap()

    _uniq = [0]

    def sbuf(name, shape, dt):
        _uniq[0] += 1
        return nc.sbuf_tensor(f"{name}_{_uniq[0]}", shape, dt)

    def psum(name, shape, dt):
        _uniq[0] += 1
        return nc.psum_tensor(f"{name}_{_uniq[0]}", shape, dt)

    xT = din("xT", [D, TL], F32)
    bWt = din("bWt", [D, D], F32)
    woT = din("woT", [D, D], F32)
    wqkA = din("wqkA", [128, NCH * 1024], F32)
    wqkB = din("wqkB", [128, NCH * 1024], F32)
    wvTi = din("wvTi", [128, NCH * 1024], F32)
    a2i = din("a2i", [128, NCH * 128], F32)
    b2i = din("b2i", [128, D], F32)
    rwi = din("rwi", [128, NCH * 15], F32)
    bbi = din("bbi", [128, NCH], F32)
    rbi = din("rbi", [1, 15], F32)
    emi = din("emi", [16, 1], I32)
    cosi = din("cosi", [64, SEQ], BF16)
    sini = din("sini", [64, SEQ], BF16)
    pshi = din("pshi", [64, 64], BF16)
    seli = din("seli", [8, 128], BF16)
    cmi = din("cmi", [128, 4 * 512], BF16)
    idi = din("idi", [128, 128], BF16)
    ioi = din("ioi", [16, 8], F32)
    m4i = din("m4i", [128, 4], F32)

    yT = nc.dram_tensor("yT", [D, TL], F32, kind="ExternalOutput").ap()

    x1T = dint("x1T", [D, TL], F32)
    bW16 = dint("bW16", [D, D], BF16)
    x2T = dint("x2T", [D, TL], F32)
    ag1i = [dint(f"ag1i{g}", [128, 8 * TL], BF16) for g in range(4)]
    ag1o = [dint(f"ag1o{g}", [512, 8 * TL], BF16) for g in range(4)]
    agsi = dint("agsi", [1, 2048], BF16)
    agso = dint("agso", [4, 2048], BF16)
    qTd = dint("qTd", [128, 8 * SEQ], BF16)
    kTd = dint("kTd", [128, 8 * SEQ], BF16)
    vTd = dint("vTd", [128, NCH * 1024], BF16)
    ag2i = [[dint(f"ag2i{h}{x}", [256, SEQ // 4], BF16) for x in range(4)]
            for h in range(NHL)]
    ag2o = [[dint(f"ag2o{h}{x}", [1024, SEQ // 4], BF16) for x in range(4)]
            for h in range(NHL)]

    tap_out = {t: nc.dram_tensor("tap_" + t, [D, TL], F32, kind="ExternalOutput").ap()
               for t in taps if t != "fdbg"}

    es = contextlib.ExitStack()
    E = es.enter_context
    p = Plan(lambda name: E(nc.semaphore(name)))

    arena = E(nc.sbuf_tensor("arena", [128, NCH * 1024], BF16))
    xq = E(nc.sbuf_tensor("xq", [128, 2 * NCH * 512], BF16))
    ident = E(nc.sbuf_tensor("ident", [128, 128], BF16))
    iota = E(nc.sbuf_tensor("iota", [16, 8], F32))
    m4 = E(nc.sbuf_tensor("m4", [128, 4], F32))
    sel = E(nc.sbuf_tensor("sel", [8, 128], BF16))
    ones_c = E(nc.sbuf_tensor("ones_c", [128, 1], BF16))
    ones_r = E(nc.sbuf_tensor("ones_r", [1, 128], BF16))
    ones_rf = E(nc.sbuf_tensor("ones_rf", [1, 128], F32))
    mapi = E(nc.sbuf_tensor("mapi", [16, 1], I32))
    mapf = E(nc.sbuf_tensor("mapf", [16, 1], F32))
    m1 = E(nc.sbuf_tensor("m1", [16, 8], BF16))
    epsb = E(nc.sbuf_tensor("epsb", [1, 1], F32))
    a2s = E(nc.sbuf_tensor("f_a2s", [128, NCH * 128], BF16))
    rws = E(nc.sbuf_tensor("f_rws", [128, NCH * 15], BF16))
    bbs = E(nc.sbuf_tensor("f_bbs", [128, NCH], F32))
    rbs = E(nc.sbuf_tensor("f_rbs", [1, 15], BF16))
    wg16 = E(nc.sbuf_tensor("f_wg16", [8, 1024], BF16))
    wbig = E(nc.sbuf_tensor("f_wbig", [128, 1024], BF16))
    hw = E(nc.sbuf_tensor("f_hw", [128, 1024], BF16))
    pu = E(nc.sbuf_tensor("f_pu", [128, 15], F32))
    cmpw = E(nc.sbuf_tensor("f_cmpw", [128, 15 * 15], BF16))
    cnt = E(nc.sbuf_tensor("f_cnt", [128, 15], F32))
    mkp = E(nc.sbuf_tensor("f_mkp", [128, 15], F32))
    s1 = E(nc.sbuf_tensor("f_s1", [128, 1], F32))
    rcp = E(nc.sbuf_tensor("f_rcp", [128, 1], F32))
    g16 = E(nc.sbuf_tensor("f_g16", [128, 16], BF16))
    gts = E(nc.sbuf_tensor("f_gts", [16, 128], BF16))
    wst_p = E(nc.sbuf_tensor("f_wstp", [128, 2 * 4096], BF16))

    xn = [arena[:, c * 1024:(c + 1) * 1024] for c in range(NCH)]
    xnq = [xq[:, c * 1024:(c + 1) * 1024] for c in range(NCH)]

    p.dma("sync", ident[:], idi, "cst")
    p.dma("sync", iota[:], ioi, "cst")
    p.dma("sync", m4[:], m4i, "cst")
    p.dma("sync", sel[:], seli, "cst")
    cP0 = p.dma("sync", mapi[:], emi, "cst")
    p.op("vector", lambda e: e.memset(ones_c[:], 1.0))
    p.op("vector", lambda e: e.memset(ones_r[:], 1.0))
    p.op("vector", lambda e: e.memset(ones_rf[:], 1.0))
    p.op("vector", lambda e: e.memset(epsb[:], 1e-5))
    p.wait("scalar", "Dcst", cP0)
    mcp = p.op("scalar", lambda e: e.copy(mapf[:], mapi[:]), "ac")
    p.wait("vector", "ac", mcp)
    p.wait("vector", "Dcst", cP0)
    v_m1 = p.op("vector", lambda e: e.tensor_single_scalar(
        m1[:], iota[:], mapf[:], ALU.is_equal), "dv")
    p.dma("gpsimd", a2s[:], a2i, "fsw")
    p.dma("gpsimd", rws[:], rwi, "fsw")
    w3P = p.dma("gpsimd", rbs[:], rbi, "fsw")
    w1P = p.dma("sync", bbs[:], bbi, "fbb")
    p.op("vector", lambda e: e.memset(g16[:, 15:16], 0.0))

    def alloc_ln(stk):
        PE = stk.enter_context
        return dict(
            xs=PE(sbuf("ln_xs", [128, 2 * 1024], F32)),
            sq=PE(sbuf("ln_sq", [128, 1024], BF16)),
            stm=PE(sbuf("ln_stm", [1, 1024], F32)),
            str_=PE(sbuf("ln_str", [1, 1024], F32)),
            mb=PE(sbuf("ln_mb", [128, 1024], F32)),
            rb=PE(sbuf("ln_rb", [128, 1024], F32)),
            psA=PE(psum("ln_psA", [128, 1024], F32)),
            psB=PE(psum("ln_psB", [128, 1024], F32)),
        )

    def ln_phase(src, xnv, arena_free_pe, pre, bufs=None, src_ready=None,
                 group_done=None):
        with contextlib.ExitStack() as ph:
            b = bufs if bufs is not None else alloc_ln(ph)
            xs, sq = b["xs"], b["sq"]
            stm, str_ = b["stm"], b["str_"]
            mb, rb, psA, psB = b["mb"], b["rb"], b["psA"], b["psB"]
            stv = mb[0:1, 0:1024]
            stt = rb[0:1, 0:1024]

            for eng in ("sync", "vector", "scalar", "tensor"):
                for s, v in pre:
                    p.wait(eng, s, v)
            p.wait("vector", "pe", arena_free_pe)
            cast = [None] * NCH
            sqd = [None] * NCH
            mmd = [None] * NCH
            for c in range(NCH):
                sl = c % 2
                if c >= 2:
                    p.wait("sync", "dv", cast[c - 2])
                if src_ready is not None:
                    s, v = src_ready(c)
                    p.wait("sync", s, v)
                ld = p.dma("sync", xs[:, sl * 1024:(sl + 1) * 1024],
                           src[c * 128:(c + 1) * 128, :], f"xs{sl}")
                p.wait("vector", f"Dxs{sl}", ld)
                cast[c] = p.op("vector", lambda e, c=c, sl=sl: e.tensor_copy(
                    xnv[c], xs[:, sl * 1024:(sl + 1) * 1024]), "dv")
                if c >= 1:
                    p.wait("scalar", "pe", mmd[c - 1])
                p.wait("scalar", "dv", cast[c])
                sqd[c] = p.op("scalar", lambda e, c=c: e.activation(
                    sq[:, 0:1024], xnv[c], AF.Square), "ac")
                p.wait("tensor", "dv", cast[c])
                st, sp = (c == 0), (c == NCH - 1)
                p.op("tensor", lambda e, c=c, st=st, sp=sp: e.matmul(
                    psA[0:1, 0:512], ones_c[:], xnv[c][:, 0:512], start=st, stop=sp))
                p.op("tensor", lambda e, c=c, st=st, sp=sp: e.matmul(
                    psA[0:1, 512:1024], ones_c[:], xnv[c][:, 512:1024], start=st, stop=sp))
                p.wait("tensor", "ac", sqd[c])
                p.op("tensor", lambda e, c=c, st=st, sp=sp: e.matmul(
                    psB[0:1, 0:512], ones_c[:], sq[:, 0:512], start=st, stop=sp))
                mmd[c] = p.op("tensor", lambda e, c=c, st=st, sp=sp: e.matmul(
                    psB[0:1, 512:1024], ones_c[:], sq[:, 512:1024], start=st, stop=sp), "pe")
            p.wait("vector", "pe", mmd[NCH - 1])
            v = p.op("vector", lambda e: e.tensor_scalar_mul(stm[:], psA[0:1, :], 1.0 / D), "dv")
            p.wait("vector", "dv", v)
            v = p.op("vector", lambda e: e.tensor_mul(stt[:], stm[:], stm[:]), "dv")
            p.wait("vector", "dv", v)
            f0 = p.op("vector", lambda e: e.scalar_tensor_tensor(
                stv[:], psB[0:1, :], 1.0 / D, stt[:], ALU.mult, ALU.subtract), "dv")
            p.wait("scalar", "dv", f0)
            f1 = p.op("scalar", lambda e: e.activation(stt[:], stv[:], AF.Sqrt, bias=epsb[:]), "ac")
            p.wait("vector", "ac", f1)
            v = p.op("vector", lambda e: e.reciprocal(str_[:], stt[:]), "dv")
            p.wait("vector", "dv", v)
            v = p.op("vector", lambda e: e.tensor_mul(stv[:], stt[:], str_[:]), "dv")
            p.wait("vector", "dv", v)
            v = p.op("vector", lambda e: e.tensor_scalar(stv[:], stv[:], -1.0, 2.0, ALU.mult, ALU.add), "dv")
            p.wait("vector", "dv", v)
            f2 = p.op("vector", lambda e: e.tensor_mul(str_[:], str_[:], stv[:]), "dv")
            p.wait("tensor", "dv", f2)
            p.op("tensor", lambda e: e.matmul(psA[:, 0:512], ones_rf[:], stm[0:1, 0:512], start=True, stop=True))
            p.op("tensor", lambda e: e.matmul(psA[:, 512:1024], ones_rf[:], stm[0:1, 512:1024], start=True, stop=True))
            p.op("tensor", lambda e: e.matmul(psB[:, 0:512], ones_rf[:], str_[0:1, 0:512], start=True, stop=True))
            f3 = p.op("tensor", lambda e: e.matmul(psB[:, 512:1024], ones_rf[:], str_[0:1, 512:1024], start=True, stop=True), "pe")
            p.wait("vector", "pe", f3)
            p.op("vector", lambda e: e.tensor_copy(mb[:], psA[:, :]))
            p.op("vector", lambda e: e.tensor_copy(rb[:], psB[:, :]))
            last = None
            chunk_dvs = []
            for c in range(NCH):
                p.op("vector", lambda e, c=c: e.tensor_sub(xnv[c], xnv[c], mb[:]))
                last = p.op("vector", lambda e, c=c: e.tensor_mul(xnv[c], xnv[c], rb[:]), "dv")
                chunk_dvs.append(last)
                if group_done is not None and c % 8 == 7:
                    group_done(c // 8, last)
            return last, chunk_dvs

    def ffn_phase(xsrc, dst, xn_ready_dv, xnv, src_slots=None, dbg=None, w16=None, xcast=None,
                  xn_chunk_dv=None, wst_free_pe=None):
        with contextlib.ExitStack() as ph:
            PE = ph.enter_context
            b2s = PE(sbuf("f_b2s", [128, D], BF16))
            dbg_sb = PE(sbuf("f_dbg", [128, 3200], F32)) if dbg is not None else None
            w3 = w3P
            w1 = w1P
            p.wait("gpsimd", "dv", xn_ready_dv)
            w2 = p.dma("gpsimd", b2s[:], b2i, "fsw")

            with contextlib.ExitStack() as pr:
                psR = pr.enter_context(psum("f_psR", [128, 16], F32))
                psT = pr.enter_context(psum("f_psT", [16, 128], BF16))
                psW = pr.enter_context(psum("f_psW", [8, 1024], F32))
                if xn_chunk_dv is None:
                    p.wait("tensor", "dv", xn_ready_dv)
                p.wait("tensor", "Dfsw", w3)
                p.wait("tensor", "Dcst", cP0)
                p.wait("tensor", "dv", v_m1)
                exp_t = tr_t = wg_t = None
                dve_t = gp_t = g16_t = None
                for tt in range(8):
                    if exp_t is not None:
                        p.wait("tensor", "ac", exp_t)
                    for d in range(NCH):
                        if tt == 0 and xn_chunk_dv is not None:
                            p.wait("tensor", "dv", xn_chunk_dv[d])
                        p.op("tensor", lambda e, d=d, tt=tt: e.matmul(
                            psR[:, 0:15], xnv[d][:, tt * 128:(tt + 1) * 128],
                            rws[:, d * 15:(d + 1) * 15], start=(d == 0), stop=False))
                    rmm = p.op("tensor", lambda e: e.matmul(
                        psR[:, 0:15], ones_r[:], rbs[:], start=False, stop=True), "pe")
                    if dbg is not None and tt == 0:
                        p.wait("vector", "pe", rmm)
                        p.op("vector", lambda e: e.tensor_copy(dbg_sb[:, 48:63], psR[:, 0:15]))
                    p.wait("scalar", "pe", rmm)
                    if dve_t is not None:
                        p.wait("scalar", "dv", dve_t)
                    exp_t = p.op("scalar", lambda e: e.activation(
                        pu[:], psR[:, 0:15], AF.Exp), "ac")
                    p.wait("vector", "ac", exp_t)
                    p.op("vector", lambda e: e.reduce_sum(s1[:], pu[:], axis=AX.X))
                    for i in range(15):
                        p.op("vector", lambda e, i=i: e.tensor_single_scalar(
                            cmpw[:, i * 15:(i + 1) * 15], pu[:], pu[:, i:i + 1], ALU.is_gt))
                    for i in range(15):
                        p.op("vector", lambda e, i=i: e.reduce_sum(
                            cnt[:, i:i + 1], cmpw[:, i * 15:(i + 1) * 15], axis=AX.X))
                    red_t = p.op("vector", lambda e: e.reciprocal(rcp[:], s1[:]), "dv")
                    if dbg is not None and tt == 0:
                        p.op("vector", lambda e: e.tensor_copy(dbg_sb[:, 0:15], pu[:]))
                        p.op("vector", lambda e: e.tensor_copy(dbg_sb[:, 16:31], cnt[:]))
                    p.wait("vector", "dv", red_t)
                    if g16_t is not None:
                        p.wait("vector", "ac", g16_t)
                    dve_t = p.op("vector", lambda e: e.scalar_tensor_tensor(
                        mkp[:], cnt[:], 7.5, pu[:], ALU.is_lt, ALU.mult), "dv")
                    gp_t = dve_t
                    p.wait("scalar", "dv", dve_t)
                    if tr_t is not None:
                        p.wait("scalar", "pe", tr_t)
                    g16_t = p.op("scalar", lambda e: e.activation(
                        g16[:, 0:15], mkp[:], AF.Copy, scale=rcp[:]), "ac")
                    p.wait("tensor", "ac", g16_t)
                    tr_t = p.op("tensor", lambda e: e.transpose(psT[:], g16[:], ident[:]), "pe")
                    if wg_t is not None:
                        p.wait("scalar", "pe", wg_t)
                    p.wait("scalar", "pe", tr_t)
                    gc = p.op("scalar", lambda e: e.copy(gts[:], psT[:]), "ac")
                    p.wait("tensor", "ac", gc)
                    wg_t = p.op("tensor", lambda e, tt=tt: e.matmul(
                        psW[0:8, tt * 128:(tt + 1) * 128], m1[:], gts[:],
                        start=True, stop=True), "pe")
                p.wait("vector", "pe", wg_t)
                wgc = p.op("vector", lambda e: e.tensor_copy(wg16[:], psW[0:8, :]), "dv")
                if dbg is not None:
                    p.op("vector", lambda e: e.tensor_copy(dbg_sb[0:8, 64:1088], wg16[:]))

            with contextlib.ExitStack() as pr:
                psB2 = pr.enter_context(psum("f_psB2", [128, 1024], F32))
                psH = pr.enter_context(psum("f_psH", [128, 1024], F32))
                p.wait("tensor", "dv", wgc)
                p.op("tensor", lambda e: e.matmul(psB2[:, 0:512], sel[:], wg16[:, 0:512], start=True, stop=True))
                wb = p.op("tensor", lambda e: e.matmul(psB2[:, 512:1024], sel[:], wg16[:, 512:1024], start=True, stop=True), "pe")
                p.wait("scalar", "pe", wb)
                wbc = p.op("scalar", lambda e: e.copy(wbig[:], psB2[:, :]), "ac")
                hm = None
                for d in range(NCH):
                    st, sp = (d == 0), (d == NCH - 1)
                    p.op("tensor", lambda e, d=d, st=st, sp=sp: e.matmul(
                        psH[:, 0:512], a2s[:, d * 128:(d + 1) * 128], xnv[d][:, 0:512], start=st, stop=sp))
                    hm = p.op("tensor", lambda e, d=d, st=st, sp=sp: e.matmul(
                        psH[:, 512:1024], a2s[:, d * 128:(d + 1) * 128], xnv[d][:, 512:1024], start=st, stop=sp),
                        "pe" if sp else None)
                p.wait("vector", "pe", hm)
                p.wait("vector", "ac", wbc)
                hwd = p.op("vector", lambda e: e.tensor_mul(hw[:], psH[:, :], wbig[:]), "dv")
                if dbg is not None:
                    p.op("vector", lambda e: e.tensor_copy(dbg_sb[:, 1088:2112], wbig[:]))
                    p.op("vector", lambda e: e.tensor_copy(dbg_sb[:, 2112:3136], hw[:]))
                    dv_ = p.op("vector", lambda e: e.tensor_copy(dbg_sb[:, 3136:3152], g16[:]), "dv")
                    p.wait("sync", "dv", dv_)
                    p.dma("sync", dbg, dbg_sb[:], "tap")

            with contextlib.ExitStack() as pr:
                PEr = pr.enter_context
                wst = wst_p
                xr = PEr(sbuf("f_xr", [128, 2 * 1024], F32))
                t1b = PEr(sbuf("f_t1b", [128, 2 * 1024], BF16))
                hbb = PEr(sbuf("f_hbb", [128, NCH], F32))
                psM = [PEr(psum(f"f_psM{i}", [128, 1024], F32)) for i in range(2)]
                p.wait("vector", "Dfbb", w1)
                hbbv = p.op("vector", lambda e: e.tensor_scalar_mul(hbb[:], bbs[:], 0.5), "dv")
                if src_slots:
                    for s in src_slots:
                        p.wait("sync", s, p.now(s))
                mm_o = [None] * NCH
                ev_o = [None] * NCH
                st_o = [None] * NCH
                cast_dv = [None] * NCH
                for o in range(NCH):
                    sl = o % 2
                    weng = "sync" if w16 is not None else "gpsimd"
                    if o >= 2:
                        p.wait(weng, "pe", mm_o[o - 2])
                    elif wst_free_pe is not None:
                        p.wait(weng, "pe", wst_free_pe)
                    if w16 is not None:
                        if o % 4 == 0:
                            p.wait("sync", f"Dpcw{o // 4}", 16)
                        wl = p.dma("sync", wst[:, sl * D:(sl + 1) * D],
                                   w16[o * 128:(o + 1) * 128, :], f"wst{sl}")
                    else:
                        wl = p.dma("gpsimd", wst[:, sl * D:(sl + 1) * D],
                                   bWt[o * 128:(o + 1) * 128, :], f"wst{sl}")
                    if o >= 2:
                        p.wait("sync", "dv", ev_o[o - 2])
                        p.wait("sync", f"Dt1{sl}", st_o[o - 2])
                    elif o == 0:
                        p.wait("sync", "dv", xn_ready_dv)
                    xl = p.dma("sync", xr[:, sl * 1024:(sl + 1) * 1024],
                               xsrc[o * 128:(o + 1) * 128, :], f"xr{sl}")
                    p.wait("tensor", f"Dwst{sl}", wl)
                    p.wait("tensor", "dv", hwd)
                    if o >= 2:
                        p.wait("tensor", "dv", ev_o[o - 2])
                    for d in range(NCH):
                        p.op("tensor", lambda e, d=d, sl=sl: e.matmul(
                            psM[sl][:, 0:512], wst[:, sl * D + d * 128:sl * D + (d + 1) * 128],
                            xnv[d][:, 0:512], start=(d == 0), stop=False))
                        p.op("tensor", lambda e, d=d, sl=sl: e.matmul(
                            psM[sl][:, 512:1024], wst[:, sl * D + d * 128:sl * D + (d + 1) * 128],
                            xnv[d][:, 512:1024], start=(d == 0), stop=False))
                    if o == 0:
                        p.wait("tensor", "Dfsw", w2)
                    p.op("tensor", lambda e, o=o, sl=sl: e.matmul(
                        psM[sl][:, 0:512], b2s[:, o * 128:(o + 1) * 128], hw[:, 0:512],
                        start=False, stop=True))
                    mm_o[o] = p.op("tensor", lambda e, o=o, sl=sl: e.matmul(
                        psM[sl][:, 512:1024], b2s[:, o * 128:(o + 1) * 128], hw[:, 512:1024],
                        start=False, stop=True), "pe")
                    p.wait("scalar", "pe", mm_o[o])
                    p.wait("scalar", "dv", hbbv)
                    evs = p.op("scalar", lambda e, o=o, sl=sl: e.activation(
                        t1b[:, sl * 1024:(sl + 1) * 1024], psM[sl][:, :],
                        AF.Identity, bias=hbb[:, o:o + 1], scale=0.5), "ac")
                    p.wait("vector", "ac", evs)
                    p.wait("vector", f"Dxr{sl}", xl)
                    ev_o[o] = p.op("vector", lambda e, sl=sl: e.tensor_add(
                        xr[:, sl * 1024:(sl + 1) * 1024],
                        t1b[:, sl * 1024:(sl + 1) * 1024],
                        xr[:, sl * 1024:(sl + 1) * 1024]), "dv")
                    p.wait("sync", "dv", ev_o[o])
                    st_o[o] = p.dma("sync", dst[o * 128:(o + 1) * 128, :],
                                    xr[:, sl * 1024:(sl + 1) * 1024], f"t1{sl}")
                    if xcast is not None:
                        xcv, gcb, pcb = xcast
                        cv = p.op("vector", lambda e, o=o, sl=sl: e.tensor_copy(
                            xcv[o], xr[:, sl * 1024:(sl + 1) * 1024]), "dv")
                        cast_dv[o] = cv
                        pcb(o, cv)
                        if o % 8 == 7:
                            gcb(o // 8, cv)
                return st_o, cast_dv

    def alloc_stats(stk):
        PE = stk.enter_context
        return dict(
            sq=PE(sbuf("st_sq", [128, 1024], BF16)),
            stm=PE(sbuf("st_stm", [1, 1024], F32)),
            str_=PE(sbuf("st_str", [1, 1024], F32)),
            stv=PE(sbuf("st_stv", [1, 1024], F32)),
            stt=PE(sbuf("st_stt", [1, 1024], F32)),
            srow=PE(sbuf("st_srow", [1, 2048], BF16)),
            psA=PE(psum("st_psA", [128, 1024], F32)),
            psB=PE(psum("st_psB", [128, 1024], F32)),
        )

    def stats_phase(xnv, chunk_dep, b):
        sq, stm, str_ = b["sq"], b["stm"], b["str_"]
        stv, stt, srow = b["stv"], b["stt"], b["srow"]
        psA, psB = b["psA"], b["psB"]
        sqd = [None] * NCH
        mmd = [None] * NCH
        for c in range(NCH):
            if c >= 1:
                p.wait("scalar", "pe", mmd[c - 1])
            p.wait("scalar", "dv", chunk_dep(c))
            sqd[c] = p.op("scalar", lambda e, c=c: e.activation(
                sq[:, 0:1024], xnv[c], AF.Square), "ac")
            p.wait("tensor", "dv", chunk_dep(c))
            st, sp = (c == 0), (c == NCH - 1)
            p.op("tensor", lambda e, c=c, st=st, sp=sp: e.matmul(
                psA[0:1, 0:512], ones_c[:], xnv[c][:, 0:512], start=st, stop=sp))
            p.op("tensor", lambda e, c=c, st=st, sp=sp: e.matmul(
                psA[0:1, 512:1024], ones_c[:], xnv[c][:, 512:1024], start=st, stop=sp))
            p.wait("tensor", "ac", sqd[c])
            p.op("tensor", lambda e, st=st, sp=sp: e.matmul(
                psB[0:1, 0:512], ones_c[:], sq[:, 0:512], start=st, stop=sp))
            mmd[c] = p.op("tensor", lambda e, st=st, sp=sp: e.matmul(
                psB[0:1, 512:1024], ones_c[:], sq[:, 512:1024], start=st, stop=sp), "pe")
        p.wait("vector", "pe", mmd[NCH - 1])
        v = p.op("vector", lambda e: e.tensor_scalar_mul(stm[:], psA[0:1, :], 1.0 / D), "dv")
        p.wait("vector", "dv", v)
        v = p.op("vector", lambda e: e.tensor_mul(stt[:], stm[:], stm[:]), "dv")
        p.wait("vector", "dv", v)
        f0 = p.op("vector", lambda e: e.scalar_tensor_tensor(
            stv[:], psB[0:1, :], 1.0 / D, stt[:], ALU.mult, ALU.subtract), "dv")
        p.wait("scalar", "dv", f0)
        f1 = p.op("scalar", lambda e: e.activation(stt[:], stv[:], AF.Sqrt, bias=epsb[:]), "ac")
        p.wait("vector", "ac", f1)
        v = p.op("vector", lambda e: e.reciprocal(str_[:], stt[:]), "dv")
        p.wait("vector", "dv", v)
        v = p.op("vector", lambda e: e.tensor_mul(stv[:], stt[:], str_[:]), "dv")
        p.wait("vector", "dv", v)
        v = p.op("vector", lambda e: e.tensor_scalar(stv[:], stv[:], -1.0, 2.0, ALU.mult, ALU.add), "dv")
        p.wait("vector", "dv", v)
        v = p.op("vector", lambda e: e.tensor_mul(str_[:], str_[:], stv[:]), "dv")
        p.wait("vector", "dv", v)
        p.op("vector", lambda e: e.tensor_copy(srow[0:1, 0:1024], stm[:]))
        v = p.op("vector", lambda e: e.tensor_copy(srow[0:1, 1024:2048], str_[:]), "dv")
        p.wait("sync", "dv", v)
        sst = p.dma("sync", agsi, srow[:], "ags")
        p.wait("gpsimd", "Dags", sst)
        return p.op("gpsimd", lambda e: e.collective_compute(
            "AllGather", ALU.bypass, replica_groups=RG,
            ins=[agsi], outs=[agso]), "ccs")

    def tap_arena(name):
        if name not in tap_out:
            return None
        with contextlib.ExitStack() as ph:
            tb = ph.enter_context(sbuf("tapb", [128, 1024], F32))
            lastst = None
            last = None
            for c in range(NCH):
                if lastst is not None:
                    p.wait("vector", "Dtap", lastst)
                last = p.op("vector", lambda e, c=c: e.tensor_copy(tb[:], xn[c]), "dv")
                p.wait("sync", "dv", last)
                lastst = p.dma("sync", tap_out[name][c * 128:(c + 1) * 128, :], tb[:], "tap")
            return last

    xn1_dv, xn1_cdv = ln_phase(xT, xn, 0, [])
    t0dv = tap_arena("xn0")
    if t0dv is not None:
        xn1_dv = t0dv
    fdbg = nc.dram_tensor("tap_fdbg", [128, 3200], F32, kind="ExternalOutput").ap() if "fdbg" in taps else None
    ag1_cc = [None] * 4
    agst = [None] * 4

    def ag1_piece(o, dvv):
        if o % 4 != 3:
            return
        g, h = o // 8, (o // 4) % 2
        p.wait("scalar", "dv", dvv)
        agst[g] = p.dma("scalar", ag1i[g][:, h * 4096:(h + 1) * 4096],
                        xq[:, (g * 8 + h * 4) * 1024:(g * 8 + h * 4 + 4) * 1024],
                        f"ag1s{g}")

    def ag1_group(g, dvv):
        p.wait("gpsimd", f"Dag1s{g}", agst[g])
        ag1_cc[g] = p.op("gpsimd", lambda e, g=g: e.collective_compute(
            "AllGather", ALU.bypass, replica_groups=RG,
            ins=[ag1i[g]], outs=[ag1o[g]]), f"cc1_{g}")

    st2_sc = contextlib.ExitStack()
    st2_bufs = alloc_stats(st2_sc)
    ffn1_st, ffn1_cast = ffn_phase(xT, x1T, xn1_dv, xn, dbg=fdbg, xn_chunk_dv=xn1_cdv,
                                   xcast=(xnq, ag1_group, ag1_piece))
    if "x1" in tap_out:
        for s in ("Dt10", "Dt11"):
            p.wait("sync", s, p.now(s))
        for c in range(NCH):
            if c >= 1:
                p.wait("sync", "Dtap", p.now("Dtap"))
            p.dma("sync", tap_out["x1"][c * 128:(c + 1) * 128, :],
                  x1T[c * 128:(c + 1) * 128, :], "tap")

    ffn1_pe = p.now("pe")
    stats_cc = stats_phase(xnq, lambda c: ffn1_cast[c], st2_bufs)
    st2_sc.close()

    p.wait("gpsimd", "pe", ffn1_pe)
    qk_wc = [p.dma("gpsimd", arena[:, c * 1024:(c + 1) * 1024],
                   wqkA[:, c * 1024:(c + 1) * 1024], f"wc{c}") for c in range(8)]
    qk_wld = [None] + [p.dma("gpsimd", arena[:, g * 8192:(g + 1) * 8192],
                             wqkA[:, g * 8192:(g + 1) * 8192], f"war{g}")
                       for g in range(1, 4)]

    qkv_sc = contextlib.ExitStack()
    cos_s = qkv_sc.enter_context(nc.sbuf_tensor("cos_s", [64, SEQ], BF16))
    sin_s = qkv_sc.enter_context(nc.sbuf_tensor("sin_s", [64, SEQ], BF16))
    pshuf_s = qkv_sc.enter_context(nc.sbuf_tensor("pshuf_s", [64, 64], BF16))
    sld = [qkv_sc.enter_context(sbuf(f"q_sld{q}", [1, 2048], BF16)) for q in range(4)]
    scol = qkv_sc.enter_context(sbuf("q_scol", [128, 32], F32))
    rbb = qkv_sc.enter_context(sbuf("q_rbb", [128, 512], F32))
    p.dma("sync", cos_s[:], cosi, "cos")
    p.dma("sync", sin_s[:], sini, "cos")
    cr = p.dma("sync", pshuf_s[:], pshi, "cos")
    p.wait("tensor", "Dcos", cr)
    p.wait("vector", "Dcos", cr)
    p.wait("sync", "ccs", stats_cc)
    for q in range(4):
        p.dma("sync", sld[q][:], agso[q:q + 1, :], "ssl")
    p.wait("gpsimd", "ccs", stats_cc)
    for q in range(4):
        for k in range(8):
            ssl = p.dma("gpsimd", scol[:, q * 8 + k:q * 8 + k + 1],
                        agso[q:q + 1, 1024 + k * 128:1024 + (k + 1) * 128].rearrange(
                            "a (p b) -> p (a b)", p=128), "ssl")

    def qkv_pass(wsrc, kind, hls, arena_free_pe, prev_dv, preload=None):
        with contextlib.ExitStack() as ph:
            PEr = ph.enter_context
            nw1p = PEr(sbuf("q_nw1p", [1, 1024], BF16))
            psW1 = PEr(psum("q_psW1", [1, 1024], F32))
            if kind == "qk":
                pc = PEr(sbuf("q_pc", [128, 4 * 512], BF16))
                ro1 = PEr(sbuf("q_ro1", [64, 512], BF16))
                ro2 = PEr(sbuf("q_ro2", [64, 512], BF16))
                psQ = [PEr(psum(f"q_psQ{i}", [128, 512], F32)) for i in range(2)]
                psSh = PEr(psum("q_psSh", [64, 512], F32))
                psRB = PEr(psum("q_psRB", [128, 512], F32))
            else:
                vp = PEr(sbuf("q_vp", [128, 2 * 1024], BF16))
                psV = [PEr(psum(f"q_psV{i}", [128, 1024], F32)) for i in range(2)]

            first_load = [True, True]
            if preload is not None:
                wc_v, wld = preload
            else:
                p.wait("gpsimd", "pe", arena_free_pe)
                wc_v = [p.dma("gpsimd", arena[:, c * 1024:(c + 1) * 1024],
                              wsrc[:, c * 1024:(c + 1) * 1024], f"wc{c}") for c in range(8)]
                wld = [None] + [p.dma("gpsimd", arena[:, g * 8192:(g + 1) * 8192],
                                      wsrc[:, g * 8192:(g + 1) * 8192], f"war{g}")
                                for g in range(1, 4)]
            p.wait("tensor", "dv", prev_dv)
            for d in range(NCH):
                if d < 8:
                    p.wait("tensor", f"Dwc{d}", wc_v[d])
                elif d % 8 == 0:
                    p.wait("tensor", f"Dwar{d // 8}", wld[d // 8])
                st_, sp_ = (d == 0), (d == NCH - 1)
                p.op("tensor", lambda e, d=d, st_=st_, sp_=sp_: e.matmul(
                    psW1[0:1, 0:512], ones_c[:], arena[:, d * 1024:d * 1024 + 512],
                    start=st_, stop=sp_))
                w1mm = p.op("tensor", lambda e, d=d, st_=st_, sp_=sp_: e.matmul(
                    psW1[0:1, 512:1024], ones_c[:], arena[:, d * 1024 + 512:(d + 1) * 1024],
                    start=st_, stop=sp_), "pe" if sp_ else None)
            p.wait("vector", "pe", w1mm)
            nw1_dv = p.op("vector", lambda e: e.tensor_scalar_mul(
                nw1p[:], psW1[0:1, :], -1.0), "dv")
            p.wait("tensor", "dv", nw1_dv)
            p.wait("tensor", "Dssl", ssl)
            xq_t = [None, None]
            last_mm = [0, 0]
            pc_store = {}
            pc_copy = {}
            rope_dv = None
            rbb_dv = None
            first_tensor = True
            for tc in range(8):
                sl = tc % 2
                blk, half = tc // 2, tc % 2
                if xq_t[sl] is not None:
                    p.wait("sync", "pe", last_mm[sl])
                elif first_load[sl]:
                    for gg in range(4):
                        p.wait("sync", f"Dag1s{gg}", agst[gg])
                    first_load[sl] = False
                xqg = [None] * 4
                for g in range(4):
                    p.wait("sync", f"cc1_{g}", ag1_cc[g])
                    src_g = ag1o[g][blk * 128:(blk + 1) * 128, :].rearrange(
                        "p (c t) -> p c t", c=8)[:, :, half * 512:(half + 1) * 512]
                    xqg[g] = p.dma(
                        "sync",
                        xq[:, sl * NCH * 512 + g * 4096:sl * NCH * 512 + (g + 1) * 4096].rearrange(
                            "p (c t) -> p c t", c=8), src_g, f"xq{sl}{g}")
                xq_t[sl] = xqg
                if kind == "qk":
                    if rbb_dv is not None:
                        p.wait("tensor", "dv", rbb_dv)
                    rbm = p.op("tensor", lambda e, blk=blk, half=half: e.matmul(
                        psRB[:, :], ones_r[:],
                        sld[blk][0:1, 1024 + half * 512:1024 + (half + 1) * 512],
                        start=True, stop=True), "pe")
                    p.wait("vector", "pe", rbm)
                    rbb_dv = p.op("vector", lambda e: e.tensor_copy(rbb[:], psRB[:, :]), "dv")
                    for j in range(8):
                        rot, rot4 = j % 2, j % 4
                        if pc_copy.get(rot) is not None:
                            p.wait("tensor", "dv", pc_copy[rot])
                        for d in range(NCH):
                            if j == 0 and d % 8 == 0:
                                p.wait("tensor", f"Dxq{sl}{d // 8}", xq_t[sl][d // 8])
                            p.op("tensor", lambda e, j=j, d=d, sl=sl, rot=rot: e.matmul(
                                psQ[rot][:, :],
                                arena[:, d * 1024 + j * 128:d * 1024 + (j + 1) * 128],
                                xq[:, sl * NCH * 512 + d * 512:sl * NCH * 512 + (d + 1) * 512],
                                start=(d == 0), stop=False))
                        mmq = p.op("tensor", lambda e, j=j, blk=blk, half=half, rot=rot: e.matmul(
                            psQ[rot][:, :], nw1p[0:1, j * 128:(j + 1) * 128],
                            sld[blk][0:1, half * 512:(half + 1) * 512],
                            start=False, stop=True), "pe")
                        last_mm[sl] = mmq
                        p.wait("vector", "pe", mmq)
                        if pc_store.get(rot4) is not None:
                            p.wait("vector", f"Dpst{rot4}", pc_store[rot4])
                        cpq = p.op("vector", lambda e, rot=rot, rot4=rot4: e.tensor_mul(
                            pc[:, rot4 * 512:(rot4 + 1) * 512], psQ[rot][:, :], rbb[:]), "dv")
                        pc_copy[rot] = cpq
                        is_q = (j < 4)
                        hl = hls[(j // 2) % 2]
                        fc = j % 2
                        dstT = qTd if is_q else kTd
                        col = (hl * 2 + fc) * SEQ + tc * 512
                        if fc == 0:
                            p.wait("tensor", "dv", cpq)
                            if rope_dv is not None:
                                p.wait("tensor", "dv", rope_dv)
                            shm = p.op("tensor", lambda e, rot4=rot4: e.matmul(
                                psSh[:, :], pshuf_s[:, :],
                                pc[0:64, rot4 * 512:(rot4 + 1) * 512],
                                start=True, stop=True), "pe")
                            p.wait("vector", "pe", shm)
                            p.op("vector", lambda e, rot4=rot4, tc=tc: e.tensor_mul(
                                ro1[:], pc[0:64, rot4 * 512:(rot4 + 1) * 512],
                                cos_s[:, tc * 512:(tc + 1) * 512]))
                            p.op("vector", lambda e, tc=tc: e.tensor_mul(
                                ro2[:], psSh[0:64, :], sin_s[:, tc * 512:(tc + 1) * 512]))
                            rope_dv = p.op("vector", lambda e, rot4=rot4: e.tensor_add(
                                pc[0:64, rot4 * 512:(rot4 + 1) * 512], ro1[:], ro2[:]), "dv")
                            p.wait("sync", "dv", rope_dv)
                        else:
                            p.wait("sync", "dv", cpq)
                        pc_store[rot4] = p.dma("sync", dstT[:, col:col + 512],
                                               pc[:, rot4 * 512:(rot4 + 1) * 512], f"pst{rot4}")
                else:
                    for sub in range(4):
                        rot = sub % 2
                        if pc_copy.get(("v", rot)) is not None:
                            p.wait("tensor", "dv", pc_copy[("v", rot)])
                        for d in range(NCH):
                            if sub == 0 and d % 8 == 0:
                                p.wait("tensor", f"Dxq{sl}{d // 8}", xq_t[sl][d // 8])
                            p.op("tensor", lambda e, d=d, sl=sl, sub=sub, rot=rot: e.matmul(
                                psV[rot][:, 0:512],
                                xq[:, sl * NCH * 512 + d * 512 + sub * 128:sl * NCH * 512 + d * 512 + (sub + 1) * 128],
                                arena[:, d * 1024:d * 1024 + 512],
                                start=(d == 0), stop=False))
                            p.op("tensor", lambda e, d=d, sl=sl, sub=sub, rot=rot: e.matmul(
                                psV[rot][:, 512:1024],
                                xq[:, sl * NCH * 512 + d * 512 + sub * 128:sl * NCH * 512 + d * 512 + (sub + 1) * 128],
                                arena[:, d * 1024 + 512:(d + 1) * 1024],
                                start=(d == 0), stop=False))
                        p.op("tensor", lambda e, blk=blk, half=half, sub=sub, rot=rot: e.matmul(
                            psV[rot][:, 0:512],
                            sld[blk][0:1, half * 512 + sub * 128:half * 512 + (sub + 1) * 128],
                            nw1p[0:1, 0:512], start=False, stop=True))
                        mmv = p.op("tensor", lambda e, blk=blk, half=half, sub=sub, rot=rot: e.matmul(
                            psV[rot][:, 512:1024],
                            sld[blk][0:1, half * 512 + sub * 128:half * 512 + (sub + 1) * 128],
                            nw1p[0:1, 512:1024], start=False, stop=True), "pe")
                        last_mm[sl] = mmv
                        p.wait("vector", "pe", mmv)
                        if pc_store.get(("v", rot)) is not None:
                            p.wait("vector", f"Dvst{rot}", pc_store[("v", rot)])
                        rix = blk * 8 + half * 4 + sub
                        cpv = p.op("vector", lambda e, rot=rot, rix=rix: e.tensor_scalar(
                            vp[:, rot * 1024:(rot + 1) * 1024], psV[rot][:, :],
                            scol[:, rix:rix + 1], None, ALU.mult), "dv")
                        pc_copy[("v", rot)] = cpv
                        p.wait("sync", "dv", cpv)
                        pc_store[("v", rot)] = p.dma("sync", vTd[:, (tc * 4 + sub) * 1024:(tc * 4 + sub + 1) * 1024],
                                                     vp[:, rot * 1024:(rot + 1) * 1024], f"vst{rot}")
            return p.now("pe"), p.now("dv")

    qk1_pe, qk1_dv = qkv_pass(wqkA, "qk", (0, 1), ffn1_pe, ffn1_cast[NCH - 1],
                              preload=(qk_wc, qk_wld))
    qk2_pe, qk2_dv = qkv_pass(wqkB, "qk", (2, 3), qk1_pe, qk1_dv)
    _, qkv_dv = qkv_pass(wvTi, "v", (), qk2_pe, qk2_dv)
    qkv_sc.close()

    ag2_cc = [[None] * 4 for _ in range(NHL)]
    at_start_pe = p.now("pe")
    sel_hist = []
    sel_sync_init = [False]
    with contextlib.ExitStack() as ph:
        PEr = ph.enter_context
        cm_s = PEr(sbuf("cm_s", [128, 4 * 512], BF16))
        PT_D = 6
        pt = PEr(sbuf("at_pt", [128, PT_D * 512], BF16))
        ssum = PEr(sbuf("at_ss", [1, 512], F32))
        rin = PEr(sbuf("at_ri", [1, 512], F32))
        rbq = PEr(sbuf("at_rb", [128, 2 * 512], F32))
        ao = PEr(sbuf("at_ao", [128, 4 * 512], BF16))
        tg = PEr(sbuf("op_tg", [128, 2 * 1024], BF16))
        psS = [PEr(psum(f"at_psS{i}", [128, 512], F32)) for i in range(2)]
        psP = [PEr(psum(f"at_psP{i}", [128, 1024], F32)) for i in range(2)]
        psU = PEr(psum("at_psU", [1, 512], F32))
        psR2 = PEr(psum("at_psR2", [128, 512], F32))
        for s in ("Dpst0", "Dpst1", "Dpst2", "Dpst3", "Dvst0", "Dvst1"):
            p.wait("sync", s, p.now(s))
        cl = p.dma("sync", cm_s[:], cmi, "cmk")

        qh = xq[:, 0:2 * SEQ]
        kh = xq[:, 2 * SEQ:4 * SEQ]
        vh3 = xq[:, 4 * SEQ:4 * SEQ + NCH * 256].rearrange("p (c f) -> p c f", c=NCH)

        sc_pe = {}
        exp_ac = {}
        gate = {}
        pv_pe = {}
        sum_pe = {}

        def emit_ag2(hl, half):
            for s in ("Dao00", "Dao01", "Dao10", "Dao11"):
                p.wait("gpsimd", s, p.now(s))
            ag2_cc[hl][half] = p.op("gpsimd", lambda e, hl=hl, half=half: e.collective_compute(
                "AllGather", ALU.bypass, replica_groups=RG,
                ins=[ag2i[hl][half]], outs=[ag2o[hl][half]]), f"c2{hl}{half}")

        sel_cc_waited = set()

        def emit_select_item(hl, half, gch, qp):
            h, piece = gch // 2, gch % 2
            r = h // 4
            if not sel_sync_init[0]:
                p.wait("gpsimd", "pe", at_start_pe)
                sel_sync_init[0] = True
            if (hl, half) not in sel_cc_waited:
                p.wait("gpsimd", f"c2{hl}{half}", ag2_cc[hl][half])
                sel_cc_waited.add((hl, half))
            if True:
                if True:
                    it = len(sel_hist)
                    sl = it % 2
                    if it >= 2:
                        p.wait("gpsimd", "dv", sel_hist[it - 2])
                    tgl = p.dma("gpsimd", tg[:, sl * 1024:(sl + 1) * 1024],
                                ag2o[hl][half][r * 256 + piece * 128:r * 256 + (piece + 1) * 128,
                                               :], f"tg{sl}")
                    p.wait("vector", f"Dtg{sl}", tgl)
                    if qp == 0:
                        sd = p.op("vector", lambda e, gch=gch, sl=sl, qp=qp: e.tensor_scalar(
                            arena[:, gch * 1024:(gch + 1) * 1024],
                            tg[:, sl * 1024:(sl + 1) * 1024], m4[:, qp:qp + 1], None, ALU.mult), "dv")
                    else:
                        sd = p.op("vector", lambda e, gch=gch, sl=sl, qp=qp: e.scalar_tensor_tensor(
                            arena[:, gch * 1024:(gch + 1) * 1024],
                            tg[:, sl * 1024:(sl + 1) * 1024], m4[:, qp:qp + 1],
                            arena[:, gch * 1024:(gch + 1) * 1024], ALU.mult, ALU.add), "dv")
                    sel_hist.append(sd)

        hq_v = {}
        hk_v = {}
        hv_v = {}

        def sel_items(hl, qps=(0, 1, 2, 3)):
            return [(hl, qp, gch, qp)
                    for qp in qps
                    for gch in range(NCH) if (gch // 2) % 4 == hl]

        sel_end2 = [None]
        for hl in range(NHL):
            p.wait("sync", "pe", p.now("pe"))
            sel_todo = sel_items(hl - 1) if hl >= 1 else []
            sel3a = sel_items(hl, (0, 1, 2)) if hl == NHL - 1 else []

            def load_chunk(c, hl=hl):
                sl4 = c % 4
                for kindname, vals in (("hq", hq_v), ("hk", hk_v), ("hv", hv_v)):
                    if (c - 4) in vals:
                        p.wait("sync", f"D{kindname}{sl4}", vals[c - 4])
                dstT = {"hq": qTd, "hk": kTd}
                for kindname, vals, buf in (("hq", hq_v, qh), ("hk", hk_v, kh)):
                    for fc in range(2):
                        v = p.dma("sync", buf[:, fc * SEQ + c * 512:fc * SEQ + (c + 1) * 512],
                                  dstT[kindname][:, (hl * 2 + fc) * SEQ + c * 512:
                                                 (hl * 2 + fc) * SEQ + (c + 1) * 512],
                                  f"D{kindname}{sl4}"[1:])
                    vals[c] = v
                hv_v[c] = p.dma(
                    "sync", vh3[:, c * 4:(c + 1) * 4, :],
                    vTd.rearrange("p (c f) -> p c f", c=NCH)[:, c * 4:(c + 1) * 4,
                                                            hl * 256:(hl + 1) * 256],
                    f"hv{sl4}")

            load_chunk(0)
            load_chunk(1)

            tiles = [(qc, kt) for qc in range(8) for kt in range(4 * qc + 4)]
            T = len(tiles)
            first_i = {qc: sum(4 * q + 4 for q in range(qc)) for qc in range(8)}
            last_i = {qc: first_i[qc] + 4 * qc + 3 for qc in range(8)}
            G = hl * T

            sscopy_dv = {}
            rd_dv = {}
            rbm_pe = {}
            aodv_dv = {}
            ao_store = {}

            seen_chunks = set()

            def emit_sc(i):
                qc, kt = tiles[i]
                gi = G + i
                rot = gi % 2
                if i == 0:
                    p.wait("tensor", "Dcmk", cl)
                    if hl == 0:
                        p.wait("tensor", "dv", qkv_dv)
                kc = kt // 4
                if kc not in seen_chunks or qc not in seen_chunks:
                    for c in {kc, qc}:
                        p.wait("tensor", f"Dhq{c % 4}", hq_v[c])
                        p.wait("tensor", f"Dhk{c % 4}", hk_v[c])
                        p.wait("tensor", f"Dhv{c % 4}", hv_v[c])
                        seen_chunks.add(c)
                if gi >= 2 and (gi - 2) in exp_ac:
                    p.wait("tensor", "ac", exp_ac[gi - 2])
                p.op("tensor", lambda e, kt=kt, qc=qc, rot=rot: e.matmul(
                    psS[rot][:, :], kh[:, kt * 128:(kt + 1) * 128],
                    qh[:, qc * 512:(qc + 1) * 512], start=True, stop=False))
                smm = p.op("tensor", lambda e, kt=kt, qc=qc, rot=rot: e.matmul(
                    psS[rot][:, :], kh[:, SEQ + kt * 128:SEQ + (kt + 1) * 128],
                    qh[:, SEQ + qc * 512:SEQ + (qc + 1) * 512], start=False, stop=True), "pe")
                sc_pe[gi] = smm
                p.wait("scalar", "pe", smm)
                if (gi - PT_D) in sum_pe:
                    p.wait("scalar", "pe", sum_pe[gi - PT_D])
                pslot = gi % PT_D
                ex = p.op("scalar", lambda e, rot=rot, pslot=pslot: e.activation(
                    pt[:, pslot * 512:(pslot + 1) * 512], psS[rot][:, :],
                    AF.Exp, scale=SC), "ac")
                exp_ac[gi] = ex
                if kt >= 4 * qc:
                    jj = kt - 4 * qc
                    p.wait("vector", "ac", ex)
                    mk = p.op("vector", lambda e, jj=jj, pslot=pslot: e.tensor_mul(
                        pt[:, pslot * 512:(pslot + 1) * 512],
                        pt[:, pslot * 512:(pslot + 1) * 512],
                        cm_s[:, jj * 512:(jj + 1) * 512]), "dv")
                    gate[gi] = ("dv", mk)
                else:
                    gate[gi] = ("ac", ex)

            def emit_pv(i):
                qc, kt = tiles[i]
                gi = G + i
                par = qc % 2
                pslot = gi % PT_D
                st, sp = (kt == 0), (kt == 4 * qc + 3)
                gs, gv = gate[gi]
                p.wait("tensor", gs, gv)
                if st and (qc - 2) in aodv_dv:
                    p.wait("tensor", "dv", aodv_dv[qc - 2])
                p.op("tensor", lambda e, kt=kt, par=par, pslot=pslot, st=st, sp=sp: e.matmul(
                    psP[par][:, 0:512], vh3[:, kt, 0:128],
                    pt[:, pslot * 512:(pslot + 1) * 512], start=st, stop=sp))
                pv_pe[gi] = p.op("tensor", lambda e, kt=kt, par=par, pslot=pslot, st=st, sp=sp: e.matmul(
                    psP[par][:, 512:1024], vh3[:, kt, 128:256],
                    pt[:, pslot * 512:(pslot + 1) * 512], start=st, stop=sp), "pe")

            def emit_sum(i):
                qc, kt = tiles[i]
                gi = G + i
                pslot = gi % PT_D
                st, sp = (kt == 0), (kt == 4 * qc + 3)
                if st and (qc - 1) in sscopy_dv:
                    p.wait("tensor", "dv", sscopy_dv[qc - 1])
                sum_pe[gi] = p.op("tensor", lambda e, pslot=pslot, st=st, sp=sp: e.matmul(
                    psU[0:1, :], ones_c[:], pt[:, pslot * 512:(pslot + 1) * 512],
                    start=st, stop=sp), "pe")
                if sp:
                    p.wait("vector", "pe", pv_pe[G + last_i[qc]])
                    p.wait("vector", "pe", sum_pe[gi])
                    v2 = p.op("vector", lambda e: e.tensor_copy(ssum[:], psU[0:1, :]), "dv")
                    sscopy_dv[qc] = v2
                    p.wait("vector", "dv", v2)
                    rd_dv[qc] = p.op("vector", lambda e: e.reciprocal(rin[:], ssum[:]), "dv")

            def emit_evac_tail(qc):
                par = qc % 2
                p.wait("tensor", "dv", rd_dv[qc])
                rbm_pe[qc] = p.op("tensor", lambda e: e.matmul(
                    psR2[:, :], ones_rf[:], rin[0:1, :], start=True, stop=True), "pe")
                p.wait("vector", "pe", rbm_pe[qc])
                for half in range(2):
                    key = (par, half)
                    if key in ao_store:
                        p.wait("vector", f"Dao{par}{half}", ao_store[key])
                p.op("vector", lambda e, par=par: e.tensor_copy(
                    rbq[:, par * 512:(par + 1) * 512], psR2[:, :]))
                p.op("vector", lambda e, par=par: e.tensor_mul(
                    ao[:, par * 1024:par * 1024 + 512], psP[par][:, 0:512],
                    rbq[:, par * 512:(par + 1) * 512]))
                aodv_dv[qc] = p.op("vector", lambda e, par=par: e.tensor_mul(
                    ao[:, par * 1024 + 512:(par + 1) * 1024], psP[par][:, 512:1024],
                    rbq[:, par * 512:(par + 1) * 512]), "dv")
                p.wait("sync", "dv", aodv_dv[qc])
                hf, off = qc // 2, (qc % 2) * 512
                ao_store[(par, 0)] = p.dma(
                    "sync", ag2i[hl][hf][0:128, off:off + 512],
                    ao[:, par * 1024:par * 1024 + 512], f"ao{par}0")
                ao_store[(par, 1)] = p.dma(
                    "sync", ag2i[hl][hf][128:256, off:off + 512],
                    ao[:, par * 1024 + 512:(par + 1) * 1024], f"ao{par}1")

            SUM_LAG = 2
            LOOK = 2
            i_sc = 0
            pending = []
            loaded = {0, 1}
            for i in range(T):
                if sel_todo and i % 4 == 1:
                    emit_select_item(*sel_todo.pop(0))
                    if not sel_todo and hl == NHL - 1:
                        sel_end2[0] = sel_hist[-1]
                qc_now = tiles[i][0]
                if qc_now + 2 <= 7 and (qc_now + 2) not in loaded:
                    load_chunk(qc_now + 2)
                    loaded.add(qc_now + 2)
                while i_sc < min(i + 1 + LOOK, T):
                    emit_sc(i_sc)
                    i_sc += 1
                emit_pv(i)
                if i - SUM_LAG >= 0:
                    emit_sum(i - SUM_LAG)
                for qc in list(pending):
                    if i >= last_i[qc] + 4:
                        emit_evac_tail(qc)
                        pending.remove(qc)
                        if qc in (1, 3, 5):
                            emit_ag2(hl, qc // 2)
                qc, kt = tiles[i]
                if i - SUM_LAG >= 0 and tiles[i - SUM_LAG][1] == 4 * tiles[i - SUM_LAG][0] + 3:
                    pending.append(tiles[i - SUM_LAG][0])
            for i in range(T - SUM_LAG, T):
                emit_sum(i)
                if tiles[i][1] == 4 * tiles[i][0] + 3:
                    pending.append(tiles[i][0])
            for qc in pending:
                emit_evac_tail(qc)
            emit_ag2(hl, 3)
            for itm in sel_todo:
                emit_select_item(*itm)
            if sel_todo and hl == NHL - 1:
                sel_end2[0] = sel_hist[-1]
            for itm in sel3a:
                emit_select_item(*itm)
        opw = [p.dma("gpsimd", wst_p[:, j * D:(j + 1) * D],
                     woT[j * 128:(j + 1) * 128, :], f"wst{j}") for j in range(2)]
        for itm in sel_items(NHL - 1, (3,)):
            emit_select_item(*itm)
        aT_dv = sel_hist[-1]
        sel_end2 = sel_end2[0]
        attn_evac_dv = aodv_dv[7]
        attn_ac = p.now("ac")

    attn_pe = p.now("pe")
    ln3_sc = contextlib.ExitStack()
    ln3_bufs = alloc_ln(ln3_sc)
    with contextlib.ExitStack() as ph:
        PEr = ph.enter_context
        wst = wst_p
        xr = PEr(sbuf("op_xr", [128, 2 * 1024], F32))
        psM = [PEr(psum(f"op_psM{i}", [128, 1024], F32)) for i in range(2)]
        p.wait("sync", "dv", aT_dv)
        mm_o = [None] * NCH
        ev_o = [None] * NCH
        st_o = [None] * NCH
        for j in range(NCH):
            sl = j % 2
            if j < 2:
                wl = opw[j]
            else:
                p.wait("gpsimd", "pe", mm_o[j - 2])
                wl = p.dma("gpsimd", wst[:, sl * D:(sl + 1) * D],
                           woT[j * 128:(j + 1) * 128, :], f"wst{sl}")
            if j >= 2:
                p.wait("sync", "dv", ev_o[j - 2])
                p.wait("sync", f"Dt1{sl}", st_o[j - 2])
            xl = p.dma("sync", xr[:, sl * 1024:(sl + 1) * 1024],
                       x1T[j * 128:(j + 1) * 128, :], f"xr{sl}")
            p.wait("tensor", f"Dwst{sl}", wl)
            p.wait("tensor", "dv", sel_end2)
            if j == 0:
                p.wait("tensor", "dv", attn_evac_dv)
                p.wait("tensor", "ac", attn_ac)
            if j >= 2:
                p.wait("tensor", "dv", ev_o[j - 2])
            mm = None
            c_order = ([c for c in range(NCH) if (c // 2) % 4 != 3]
                       + [c for c in range(NCH) if (c // 2) % 4 == 3])
            for idx, c in enumerate(c_order):
                if idx == NCH - 8:
                    p.wait("tensor", "dv", aT_dv)
                p.op("tensor", lambda e, c=c, sl=sl, idx=idx: e.matmul(
                    psM[sl][:, 0:512], wst[:, sl * D + c * 128:sl * D + (c + 1) * 128],
                    arena[:, c * 1024:c * 1024 + 512], start=(idx == 0), stop=(idx == NCH - 1)))
                mm = p.op("tensor", lambda e, c=c, sl=sl, idx=idx: e.matmul(
                    psM[sl][:, 512:1024], wst[:, sl * D + c * 128:sl * D + (c + 1) * 128],
                    arena[:, c * 1024 + 512:(c + 1) * 1024], start=(idx == 0), stop=(idx == NCH - 1)),
                    "pe" if idx == NCH - 1 else None)
            mm_o[j] = mm
            p.wait("vector", "pe", mm_o[j])
            p.wait("vector", f"Dxr{sl}", xl)
            ev_o[j] = p.op("vector", lambda e, sl=sl: e.tensor_add(
                xr[:, sl * 1024:(sl + 1) * 1024], psM[sl][:, :],
                xr[:, sl * 1024:(sl + 1) * 1024]), "dv")
            p.wait("sync", "dv", ev_o[j])
            st_o[j] = p.dma("sync", x2T[j * 128:(j + 1) * 128, :],
                            xr[:, sl * 1024:(sl + 1) * 1024], f"t1{sl}")
        op_st = st_o

    if "x2" in tap_out:
        for s in ("Dt10", "Dt11"):
            p.wait("sync", s, p.now(s))
        for c in range(NCH):
            if c >= 1:
                p.wait("sync", "Dtap", p.now("Dtap"))
            p.dma("sync", tap_out["x2"][c * 128:(c + 1) * 128, :],
                  x2T[c * 128:(c + 1) * 128, :], "tap")

    xn3_dv, xn3_cdv = ln_phase(x2T, xnq, attn_pe, [], bufs=ln3_bufs,
                      src_ready=lambda c: (f"Dt1{c % 2}", op_st[c]))
    ln3_sc.close()
    ffn_phase(x2T, yT, xn3_dv, xnq, src_slots=("Dt10", "Dt11"), xn_chunk_dv=xn3_cdv,
              wst_free_pe=p.now("pe"))[0]

    p.wait("sync", "Dt10", p.now("Dt10"))
    p.wait("sync", "Dt11", p.now("Dt11"))
    if taps:
        p.wait("sync", "Dtap", p.now("Dtap"))

    p.emit(nc)
    es.close()
    return nc


def _consts():
    inv = 10000.0 ** (-np.arange(0, 32, dtype=np.float64) * 2.0 / 64.0)
    pos = np.arange(SEQ, dtype=np.float64)
    ang = pos[None, :] * inv[:, None]
    cos2 = np.repeat(np.cos(ang), 2, axis=0)
    sin2 = np.repeat(np.sin(ang), 2, axis=0)
    psh = np.zeros((64, 64), np.float32)
    for i in range(32):
        psh[2 * i + 1, 2 * i] = -1.0
        psh[2 * i, 2 * i + 1] = 1.0
    selm = np.zeros((8, 128), np.float32)
    for e_ in range(8):
        selm[e_, e_ * 16:(e_ + 1) * 16] = 2.0
    cm = np.zeros((128, 4 * 512), np.float32)
    kt = np.arange(128)[:, None]
    qt = np.arange(512)[None, :]
    for j in range(4):
        cm[:, j * 512:(j + 1) * 512] = (qt >= (j * 128 + kt)).astype(np.float32)
    return {
        "cosi": cos2.astype(bf16),
        "sini": sin2.astype(bf16),
        "pshi": psh.astype(bf16),
        "seli": selm.astype(bf16),
        "cmi": cm.astype(bf16),
        "idi": np.eye(128, dtype=np.float32).astype(bf16),
        "ioi": np.tile(np.arange(8, dtype=np.float32), (16, 1)),
    }


def _blk(w):
    return np.ascontiguousarray(
        w.reshape(32, 128, 32, 128).transpose(0, 3, 2, 1).reshape(4096, 4096))


def _wrows(m):
    return np.ascontiguousarray(
        m.reshape(1024, 32, 128).transpose(2, 1, 0).reshape(128, 32 * 1024))


def host_inputs(L0_attn_hidden_states, L0_attn_w_q, L0_attn_w_k, L0_attn_w_v,
                L0_attn_w_out, L0_ffn_base_W, L0_ffn_base_b, L0_ffn_router_W,
                L0_ffn_router_b, L0_ffn_A, L0_ffn_B, L0_ffn_expert_map):
    x = np.asarray(L0_attn_hidden_states, np.float32)
    wq = np.asarray(L0_attn_w_q, np.float32)
    wk = np.asarray(L0_attn_w_k, np.float32)
    wv = np.asarray(L0_attn_w_v, np.float32)
    wo = np.asarray(L0_attn_w_out, np.float32)
    bW = np.asarray(L0_ffn_base_W, np.float32)
    bb = np.asarray(L0_ffn_base_b, np.float32)
    rW = np.asarray(L0_ffn_router_W, np.float32)
    rb = np.asarray(L0_ffn_router_b, np.float32)
    A = np.asarray(L0_ffn_A, np.float32)
    B = np.asarray(L0_ffn_B, np.float32)
    em = np.asarray(L0_ffn_expert_map, np.int32)

    consts = _consts()
    shared = {
        "bWt": _blk(bW),
        "woT": _blk(wo),
        "a2i": np.ascontiguousarray(
            A.reshape(128, D).T.reshape(32, 128, 128).transpose(1, 0, 2).reshape(128, 32 * 128)),
        "b2i": np.ascontiguousarray(B.transpose(0, 2, 1).reshape(128, D)),
        "rwi": np.ascontiguousarray(
            rW.T.reshape(32, 128, 15).transpose(1, 0, 2).reshape(128, 32 * 15)),
        "bbi": np.ascontiguousarray(bb.reshape(32, 128).T),
        "rbi": np.ascontiguousarray(rb.reshape(1, 15)),
        "emi": np.concatenate([em, np.array([-1], np.int32)]).reshape(16, 1),
    }
    in_maps = []
    for c in range(8):
        b, q = c // 4, c % 4
        h0 = 4 * q
        xs = np.ascontiguousarray(x[b].T[:, q * TL:(q + 1) * TL])
        m4a = np.zeros((128, 4), np.float32)
        m4a[:, q] = 1.0
        wqk_a = np.concatenate([wq[h0 * 256:(h0 + 2) * 256],
                                wk[h0 * 256:(h0 + 2) * 256]], 0)
        wqk_b = np.concatenate([wq[(h0 + 2) * 256:(h0 + 4) * 256],
                                wk[(h0 + 2) * 256:(h0 + 4) * 256]], 0)
        wv_c = wv[h0 * 256:(h0 + 4) * 256]
        im = dict(shared)
        im.update(consts)
        im.update({
            "xT": xs,
            "m4i": m4a,
            "wqkA": _wrows(wqk_a),
            "wqkB": _wrows(wqk_b),
            "wvTi": _wrows(wv_c),
        })
        in_maps.append(im)
    return in_maps


def kernel(**inputs):
    in_maps = host_inputs(**inputs)
    nc = build_program()
    res = run_bass_kernel_spmd(nc, in_maps, core_ids=list(range(8)))
    out = np.empty((2, SEQ, D), np.float32)
    for c in range(8):
        b, q = c // 4, c % 4
        out[b, q * TL:(q + 1) * TL, :] = res.results[c]["yT"].T
    return out


# version 31 (final)
# speedup vs baseline: 1.0144x; 1.0144x over previous
import contextlib
import numpy as np
import ml_dtypes
import concourse.bass as bass
import concourse.mybir as mybir
from concourse.bass_utils import run_bass_kernel_spmd

F32 = mybir.dt.float32
BF16 = mybir.dt.bfloat16
I32 = mybir.dt.int32
AF = mybir.ActivationFunctionType
ALU = mybir.AluOpType
AX = mybir.AxisListType
bf16 = ml_dtypes.bfloat16

D = 4096
TL = 1024
SEQ = 4096
NCH = 32
NHL = 4
SC = 0.0625
RG = [[0, 1, 2, 3], [4, 5, 6, 7]]

ENGS = ("sync", "gpsimd", "vector", "scalar", "tensor")


class Plan:

    def __init__(self, mksem):
        self.q = {e: [] for e in ENGS}
        self.ctr = {}
        self.sem = {}
        self.mksem = mksem

    def _get(self, name):
        if name not in self.sem:
            self.sem[name] = self.mksem(name)
            self.ctr[name] = 0
        return self.sem[name]

    def now(self, s):
        self._get(s)
        return self.ctr[s]

    def op(self, eng, mk, sem=None, n=1):
        if sem is None:
            self.q[eng].append(mk)
            return None
        h = self._get(sem)
        self.q[eng].append(lambda e, mk=mk, h=h, n=n: mk(e).then_inc(h, n))
        self.ctr[sem] += n
        return self.ctr[sem]

    def wait(self, eng, sem, val):
        if val is None or val <= 0:
            return
        h = self._get(sem)
        assert val <= self.ctr[sem], f"wait {sem}>={val} but only {self.ctr[sem]} planned"
        self.q[eng].append(lambda e, h=h, v=val: e.wait_ge(h, v))

    def dma(self, eng, out, in_, slot):
        return self.op(eng, lambda e, o=out, i=in_: e.dma_start(out=o, in_=i),
                       "D" + slot, 16)

    def emit(self, nc):
        with nc.Block() as block:
            @block.sync
            def _(e):
                for f in self.q["sync"]:
                    f(e)

            @block.gpsimd
            def _(e):
                for f in self.q["gpsimd"]:
                    f(e)

            @block.vector
            def _(e):
                for f in self.q["vector"]:
                    f(e)

            @block.scalar
            def _(e):
                for f in self.q["scalar"]:
                    f(e)

            @block.tensor
            def _(e):
                for f in self.q["tensor"]:
                    f(e)


def build_program(taps=()):
    nc = bass.Bass("TRN2", target_bir_lowering=False, debug=False)

    def din(name, shape, dt):
        return nc.dram_tensor(name, shape, dt, kind="ExternalInput").ap()

    def dint(name, shape, dt):
        return nc.dram_tensor(name, shape, dt, kind="Internal").ap()

    _uniq = [0]

    def sbuf(name, shape, dt):
        _uniq[0] += 1
        return nc.sbuf_tensor(f"{name}_{_uniq[0]}", shape, dt)

    def psum(name, shape, dt):
        _uniq[0] += 1
        return nc.psum_tensor(f"{name}_{_uniq[0]}", shape, dt)

    xT = din("xT", [D, TL], F32)
    bWt = din("bWt", [D, D], F32)
    woT = din("woT", [D, D], F32)
    wqkA = din("wqkA", [128, NCH * 1024], F32)
    wqkB = din("wqkB", [128, NCH * 1024], F32)
    wvTi = din("wvTi", [128, NCH * 1024], F32)
    a2i = din("a2i", [128, NCH * 128], F32)
    b2i = din("b2i", [128, D], F32)
    rwi = din("rwi", [128, NCH * 15], F32)
    bbi = din("bbi", [128, NCH], F32)
    rbi = din("rbi", [1, 15], F32)
    emi = din("emi", [16, 1], I32)
    cosi = din("cosi", [64, SEQ], BF16)
    sini = din("sini", [64, SEQ], BF16)
    pshi = din("pshi", [64, 64], BF16)
    seli = din("seli", [8, 128], BF16)
    cmi = din("cmi", [128, 4 * 512], BF16)
    idi = din("idi", [128, 128], BF16)
    ioi = din("ioi", [16, 8], F32)
    m4i = din("m4i", [128, 4], F32)

    yT = nc.dram_tensor("yT", [D, TL], F32, kind="ExternalOutput").ap()

    x1T = dint("x1T", [D, TL], F32)
    bW16 = dint("bW16", [D, D], BF16)
    x2T = dint("x2T", [D, TL], F32)
    ag1i = [dint(f"ag1i{g}", [128, 8 * TL], BF16) for g in range(4)]
    ag1o = [dint(f"ag1o{g}", [512, 8 * TL], BF16) for g in range(4)]
    agsi = dint("agsi", [1, 2048], BF16)
    agso = dint("agso", [4, 2048], BF16)
    qTd = dint("qTd", [128, 8 * SEQ], BF16)
    kTd = dint("kTd", [128, 8 * SEQ], BF16)
    vTd = dint("vTd", [128, NCH * 1024], BF16)
    ag2i = [[dint(f"ag2i{h}{x}", [256, SEQ // 4], BF16) for x in range(4)]
            for h in range(NHL)]
    ag2o = [[dint(f"ag2o{h}{x}", [1024, SEQ // 4], BF16) for x in range(4)]
            for h in range(NHL)]

    tap_out = {t: nc.dram_tensor("tap_" + t, [D, TL], F32, kind="ExternalOutput").ap()
               for t in taps if t != "fdbg"}

    es = contextlib.ExitStack()
    E = es.enter_context
    p = Plan(lambda name: E(nc.semaphore(name)))

    arena = E(nc.sbuf_tensor("arena", [128, NCH * 1024], BF16))
    xq = E(nc.sbuf_tensor("xq", [128, 2 * NCH * 512], BF16))
    ident = E(nc.sbuf_tensor("ident", [128, 128], BF16))
    iota = E(nc.sbuf_tensor("iota", [16, 8], F32))
    m4 = E(nc.sbuf_tensor("m4", [128, 4], F32))
    sel = E(nc.sbuf_tensor("sel", [8, 128], BF16))
    ones_c = E(nc.sbuf_tensor("ones_c", [128, 1], BF16))
    ones_r = E(nc.sbuf_tensor("ones_r", [1, 128], BF16))
    ones_rf = E(nc.sbuf_tensor("ones_rf", [1, 128], F32))
    mapi = E(nc.sbuf_tensor("mapi", [16, 1], I32))
    mapf = E(nc.sbuf_tensor("mapf", [16, 1], F32))
    m1 = E(nc.sbuf_tensor("m1", [16, 8], BF16))
    epsb = E(nc.sbuf_tensor("epsb", [1, 1], F32))
    a2s = E(nc.sbuf_tensor("f_a2s", [128, NCH * 128], BF16))
    rws = E(nc.sbuf_tensor("f_rws", [128, NCH * 15], BF16))
    bbs = E(nc.sbuf_tensor("f_bbs", [128, NCH], F32))
    rbs = E(nc.sbuf_tensor("f_rbs", [1, 15], BF16))
    wg16 = E(nc.sbuf_tensor("f_wg16", [8, 1024], BF16))
    wbig = E(nc.sbuf_tensor("f_wbig", [128, 1024], BF16))
    hw = E(nc.sbuf_tensor("f_hw", [128, 1024], BF16))
    pu = E(nc.sbuf_tensor("f_pu", [128, 15], F32))
    cmpw = E(nc.sbuf_tensor("f_cmpw", [128, 15 * 15], BF16))
    cnt = E(nc.sbuf_tensor("f_cnt", [128, 15], F32))
    mkp = E(nc.sbuf_tensor("f_mkp", [128, 15], F32))
    s1 = E(nc.sbuf_tensor("f_s1", [128, 1], F32))
    rcp = E(nc.sbuf_tensor("f_rcp", [128, 1], F32))
    g16 = E(nc.sbuf_tensor("f_g16", [128, 16], BF16))
    gts = E(nc.sbuf_tensor("f_gts", [16, 128], BF16))
    wst_p = E(nc.sbuf_tensor("f_wstp", [128, 2 * 4096], BF16))

    xn = [arena[:, c * 1024:(c + 1) * 1024] for c in range(NCH)]
    xnq = [xq[:, c * 1024:(c + 1) * 1024] for c in range(NCH)]

    p.dma("sync", ident[:], idi, "cst")
    p.dma("sync", iota[:], ioi, "cst")
    p.dma("sync", m4[:], m4i, "cst")
    p.dma("sync", sel[:], seli, "cst")
    cP0 = p.dma("sync", mapi[:], emi, "cst")
    p.op("vector", lambda e: e.memset(ones_c[:], 1.0))
    p.op("vector", lambda e: e.memset(ones_r[:], 1.0))
    p.op("vector", lambda e: e.memset(ones_rf[:], 1.0))
    p.op("vector", lambda e: e.memset(epsb[:], 1e-5))
    p.wait("scalar", "Dcst", cP0)
    mcp = p.op("scalar", lambda e: e.copy(mapf[:], mapi[:]), "ac")
    p.wait("vector", "ac", mcp)
    p.wait("vector", "Dcst", cP0)
    v_m1 = p.op("vector", lambda e: e.tensor_single_scalar(
        m1[:], iota[:], mapf[:], ALU.is_equal), "dv")
    p.dma("gpsimd", a2s[:], a2i, "fsw")
    p.dma("gpsimd", rws[:], rwi, "fsw")
    w3P = p.dma("gpsimd", rbs[:], rbi, "fsw")
    w1P = p.dma("sync", bbs[:], bbi, "fbb")
    p.op("vector", lambda e: e.memset(g16[:, 15:16], 0.0))

    def alloc_ln(stk):
        PE = stk.enter_context
        return dict(
            xs=PE(sbuf("ln_xs", [128, 2 * 1024], F32)),
            sq=PE(sbuf("ln_sq", [128, 1024], BF16)),
            stm=PE(sbuf("ln_stm", [1, 1024], F32)),
            str_=PE(sbuf("ln_str", [1, 1024], F32)),
            mb=PE(sbuf("ln_mb", [128, 1024], F32)),
            rb=PE(sbuf("ln_rb", [128, 1024], F32)),
            psA=PE(psum("ln_psA", [128, 1024], F32)),
            psB=PE(psum("ln_psB", [128, 1024], F32)),
        )

    def ln_phase(src, xnv, arena_free_pe, pre, bufs=None, src_ready=None,
                 group_done=None):
        with contextlib.ExitStack() as ph:
            b = bufs if bufs is not None else alloc_ln(ph)
            xs, sq = b["xs"], b["sq"]
            stm, str_ = b["stm"], b["str_"]
            mb, rb, psA, psB = b["mb"], b["rb"], b["psA"], b["psB"]
            stv = mb[0:1, 0:1024]
            stt = rb[0:1, 0:1024]

            for eng in ("sync", "vector", "scalar", "tensor"):
                for s, v in pre:
                    p.wait(eng, s, v)
            p.wait("vector", "pe", arena_free_pe)
            cast = [None] * NCH
            sqd = [None] * NCH
            mmd = [None] * NCH
            for c in range(NCH):
                sl = c % 2
                if c >= 2:
                    p.wait("sync", "dv", cast[c - 2])
                if src_ready is not None:
                    s, v = src_ready(c)
                    p.wait("sync", s, v)
                ld = p.dma("sync", xs[:, sl * 1024:(sl + 1) * 1024],
                           src[c * 128:(c + 1) * 128, :], f"xs{sl}")
                p.wait("vector", f"Dxs{sl}", ld)
                cast[c] = p.op("vector", lambda e, c=c, sl=sl: e.tensor_copy(
                    xnv[c], xs[:, sl * 1024:(sl + 1) * 1024]), "dv")
                if c >= 1:
                    p.wait("scalar", "pe", mmd[c - 1])
                p.wait("scalar", "dv", cast[c])
                sqd[c] = p.op("scalar", lambda e, c=c: e.activation(
                    sq[:, 0:1024], xnv[c], AF.Square), "ac")
                p.wait("tensor", "dv", cast[c])
                st, sp = (c == 0), (c == NCH - 1)
                p.op("tensor", lambda e, c=c, st=st, sp=sp: e.matmul(
                    psA[0:1, 0:512], ones_c[:], xnv[c][:, 0:512], start=st, stop=sp))
                p.op("tensor", lambda e, c=c, st=st, sp=sp: e.matmul(
                    psA[0:1, 512:1024], ones_c[:], xnv[c][:, 512:1024], start=st, stop=sp))
                p.wait("tensor", "ac", sqd[c])
                p.op("tensor", lambda e, c=c, st=st, sp=sp: e.matmul(
                    psB[0:1, 0:512], ones_c[:], sq[:, 0:512], start=st, stop=sp))
                mmd[c] = p.op("tensor", lambda e, c=c, st=st, sp=sp: e.matmul(
                    psB[0:1, 512:1024], ones_c[:], sq[:, 512:1024], start=st, stop=sp), "pe")
            p.wait("vector", "pe", mmd[NCH - 1])
            v = p.op("vector", lambda e: e.tensor_scalar_mul(stm[:], psA[0:1, :], 1.0 / D), "dv")
            p.wait("vector", "dv", v)
            v = p.op("vector", lambda e: e.tensor_mul(stt[:], stm[:], stm[:]), "dv")
            p.wait("vector", "dv", v)
            f0 = p.op("vector", lambda e: e.scalar_tensor_tensor(
                stv[:], psB[0:1, :], 1.0 / D, stt[:], ALU.mult, ALU.subtract), "dv")
            p.wait("scalar", "dv", f0)
            f1 = p.op("scalar", lambda e: e.activation(stt[:], stv[:], AF.Sqrt, bias=epsb[:]), "ac")
            p.wait("vector", "ac", f1)
            v = p.op("vector", lambda e: e.reciprocal(str_[:], stt[:]), "dv")
            p.wait("vector", "dv", v)
            v = p.op("vector", lambda e: e.tensor_mul(stv[:], stt[:], str_[:]), "dv")
            p.wait("vector", "dv", v)
            v = p.op("vector", lambda e: e.tensor_scalar(stv[:], stv[:], -1.0, 2.0, ALU.mult, ALU.add), "dv")
            p.wait("vector", "dv", v)
            f2 = p.op("vector", lambda e: e.tensor_mul(str_[:], str_[:], stv[:]), "dv")
            p.wait("tensor", "dv", f2)
            p.op("tensor", lambda e: e.matmul(psA[:, 0:512], ones_rf[:], stm[0:1, 0:512], start=True, stop=True))
            p.op("tensor", lambda e: e.matmul(psA[:, 512:1024], ones_rf[:], stm[0:1, 512:1024], start=True, stop=True))
            p.op("tensor", lambda e: e.matmul(psB[:, 0:512], ones_rf[:], str_[0:1, 0:512], start=True, stop=True))
            f3 = p.op("tensor", lambda e: e.matmul(psB[:, 512:1024], ones_rf[:], str_[0:1, 512:1024], start=True, stop=True), "pe")
            p.wait("vector", "pe", f3)
            p.op("vector", lambda e: e.tensor_copy(mb[:], psA[:, :]))
            p.op("vector", lambda e: e.tensor_copy(rb[:], psB[:, :]))
            last = None
            chunk_dvs = []
            for c in range(NCH):
                p.op("vector", lambda e, c=c: e.tensor_sub(xnv[c], xnv[c], mb[:]))
                last = p.op("vector", lambda e, c=c: e.tensor_mul(xnv[c], xnv[c], rb[:]), "dv")
                chunk_dvs.append(last)
                if group_done is not None and c % 8 == 7:
                    group_done(c // 8, last)
            return last, chunk_dvs

    def ffn_phase(xsrc, dst, xn_ready_dv, xnv, src_slots=None, dbg=None, w16=None, xcast=None,
                  xn_chunk_dv=None, wst_free_pe=None):
        with contextlib.ExitStack() as ph:
            PE = ph.enter_context
            b2s = PE(sbuf("f_b2s", [128, D], BF16))
            dbg_sb = PE(sbuf("f_dbg", [128, 3200], F32)) if dbg is not None else None
            w3 = w3P
            w1 = w1P
            p.wait("gpsimd", "dv", xn_ready_dv)
            w2 = p.dma("gpsimd", b2s[:], b2i, "fsw")

            with contextlib.ExitStack() as pr:
                psR = pr.enter_context(psum("f_psR", [128, 16], F32))
                psT = pr.enter_context(psum("f_psT", [16, 128], BF16))
                psW = pr.enter_context(psum("f_psW", [8, 1024], F32))
                if xn_chunk_dv is None:
                    p.wait("tensor", "dv", xn_ready_dv)
                p.wait("tensor", "Dfsw", w3)
                p.wait("tensor", "Dcst", cP0)
                p.wait("tensor", "dv", v_m1)
                exp_t = tr_t = wg_t = None
                dve_t = gp_t = g16_t = None
                for tt in range(8):
                    if exp_t is not None:
                        p.wait("tensor", "ac", exp_t)
                    for d in range(NCH):
                        if tt == 0 and xn_chunk_dv is not None:
                            p.wait("tensor", "dv", xn_chunk_dv[d])
                        p.op("tensor", lambda e, d=d, tt=tt: e.matmul(
                            psR[:, 0:15], xnv[d][:, tt * 128:(tt + 1) * 128],
                            rws[:, d * 15:(d + 1) * 15], start=(d == 0), stop=False))
                    rmm = p.op("tensor", lambda e: e.matmul(
                        psR[:, 0:15], ones_r[:], rbs[:], start=False, stop=True), "pe")
                    if dbg is not None and tt == 0:
                        p.wait("vector", "pe", rmm)
                        p.op("vector", lambda e: e.tensor_copy(dbg_sb[:, 48:63], psR[:, 0:15]))
                    p.wait("scalar", "pe", rmm)
                    if dve_t is not None:
                        p.wait("scalar", "dv", dve_t)
                    exp_t = p.op("scalar", lambda e: e.activation(
                        pu[:], psR[:, 0:15], AF.Exp), "ac")
                    p.wait("vector", "ac", exp_t)
                    p.op("vector", lambda e: e.reduce_sum(s1[:], pu[:], axis=AX.X))
                    for i in range(15):
                        p.op("vector", lambda e, i=i: e.tensor_single_scalar(
                            cmpw[:, i * 15:(i + 1) * 15], pu[:], pu[:, i:i + 1], ALU.is_gt))
                    for i in range(15):
                        p.op("vector", lambda e, i=i: e.reduce_sum(
                            cnt[:, i:i + 1], cmpw[:, i * 15:(i + 1) * 15], axis=AX.X))
                    red_t = p.op("vector", lambda e: e.reciprocal(rcp[:], s1[:]), "dv")
                    if dbg is not None and tt == 0:
                        p.op("vector", lambda e: e.tensor_copy(dbg_sb[:, 0:15], pu[:]))
                        p.op("vector", lambda e: e.tensor_copy(dbg_sb[:, 16:31], cnt[:]))
                    p.wait("vector", "dv", red_t)
                    if g16_t is not None:
                        p.wait("vector", "ac", g16_t)
                    dve_t = p.op("vector", lambda e: e.scalar_tensor_tensor(
                        mkp[:], cnt[:], 7.5, pu[:], ALU.is_lt, ALU.mult), "dv")
                    gp_t = dve_t
                    p.wait("scalar", "dv", dve_t)
                    if tr_t is not None:
                        p.wait("scalar", "pe", tr_t)
                    g16_t = p.op("scalar", lambda e: e.activation(
                        g16[:, 0:15], mkp[:], AF.Copy, scale=rcp[:]), "ac")
                    p.wait("tensor", "ac", g16_t)
                    tr_t = p.op("tensor", lambda e: e.transpose(psT[:], g16[:], ident[:]), "pe")
                    if wg_t is not None:
                        p.wait("scalar", "pe", wg_t)
                    p.wait("scalar", "pe", tr_t)
                    gc = p.op("scalar", lambda e: e.copy(gts[:], psT[:]), "ac")
                    p.wait("tensor", "ac", gc)
                    wg_t = p.op("tensor", lambda e, tt=tt: e.matmul(
                        psW[0:8, tt * 128:(tt + 1) * 128], m1[:], gts[:],
                        start=True, stop=True), "pe")
                p.wait("vector", "pe", wg_t)
                wgc = p.op("vector", lambda e: e.tensor_copy(wg16[:], psW[0:8, :]), "dv")
                if dbg is not None:
                    p.op("vector", lambda e: e.tensor_copy(dbg_sb[0:8, 64:1088], wg16[:]))

            with contextlib.ExitStack() as pr:
                psB2 = pr.enter_context(psum("f_psB2", [128, 1024], F32))
                psH = pr.enter_context(psum("f_psH", [128, 1024], F32))
                p.wait("tensor", "dv", wgc)
                p.op("tensor", lambda e: e.matmul(psB2[:, 0:512], sel[:], wg16[:, 0:512], start=True, stop=True))
                wb = p.op("tensor", lambda e: e.matmul(psB2[:, 512:1024], sel[:], wg16[:, 512:1024], start=True, stop=True), "pe")
                p.wait("scalar", "pe", wb)
                wbc = p.op("scalar", lambda e: e.copy(wbig[:], psB2[:, :]), "ac")
                hm = None
                for d in range(NCH):
                    st, sp = (d == 0), (d == NCH - 1)
                    p.op("tensor", lambda e, d=d, st=st, sp=sp: e.matmul(
                        psH[:, 0:512], a2s[:, d * 128:(d + 1) * 128], xnv[d][:, 0:512], start=st, stop=sp))
                    hm = p.op("tensor", lambda e, d=d, st=st, sp=sp: e.matmul(
                        psH[:, 512:1024], a2s[:, d * 128:(d + 1) * 128], xnv[d][:, 512:1024], start=st, stop=sp),
                        "pe" if sp else None)
                p.wait("vector", "pe", hm)
                p.wait("vector", "ac", wbc)
                hwd = p.op("vector", lambda e: e.tensor_mul(hw[:], psH[:, :], wbig[:]), "dv")
                if dbg is not None:
                    p.op("vector", lambda e: e.tensor_copy(dbg_sb[:, 1088:2112], wbig[:]))
                    p.op("vector", lambda e: e.tensor_copy(dbg_sb[:, 2112:3136], hw[:]))
                    dv_ = p.op("vector", lambda e: e.tensor_copy(dbg_sb[:, 3136:3152], g16[:]), "dv")
                    p.wait("sync", "dv", dv_)
                    p.dma("sync", dbg, dbg_sb[:], "tap")

            with contextlib.ExitStack() as pr:
                PEr = pr.enter_context
                wst = wst_p
                xr = PEr(sbuf("f_xr", [128, 2 * 1024], F32))
                t1b = PEr(sbuf("f_t1b", [128, 2 * 1024], BF16))
                hbb = PEr(sbuf("f_hbb", [128, NCH], F32))
                psM = [PEr(psum(f"f_psM{i}", [128, 1024], F32)) for i in range(2)]
                p.wait("vector", "Dfbb", w1)
                hbbv = p.op("vector", lambda e: e.tensor_scalar_mul(hbb[:], bbs[:], 0.5), "dv")
                if src_slots:
                    for s in src_slots:
                        p.wait("sync", s, p.now(s))
                mm_o = [None] * NCH
                ev_o = [None] * NCH
                st_o = [None] * NCH
                cast_dv = [None] * NCH
                for o in range(NCH):
                    sl = o % 2
                    weng = "sync" if w16 is not None else "gpsimd"
                    if o >= 2:
                        p.wait(weng, "pe", mm_o[o - 2])
                    elif wst_free_pe is not None:
                        p.wait(weng, "pe", wst_free_pe)
                    if w16 is not None:
                        if o % 4 == 0:
                            p.wait("sync", f"Dpcw{o // 4}", 16)
                        wl = p.dma("sync", wst[:, sl * D:(sl + 1) * D],
                                   w16[o * 128:(o + 1) * 128, :], f"wst{sl}")
                    else:
                        wl = p.dma("gpsimd", wst[:, sl * D:(sl + 1) * D],
                                   bWt[o * 128:(o + 1) * 128, :], f"wst{sl}")
                    if o >= 2:
                        p.wait("sync", "dv", ev_o[o - 2])
                        p.wait("sync", f"Dt1{sl}", st_o[o - 2])
                    elif o == 0:
                        p.wait("sync", "dv", xn_ready_dv)
                    xl = p.dma("sync", xr[:, sl * 1024:(sl + 1) * 1024],
                               xsrc[o * 128:(o + 1) * 128, :], f"xr{sl}")
                    p.wait("tensor", f"Dwst{sl}", wl)
                    p.wait("tensor", "dv", hwd)
                    if o >= 2:
                        p.wait("tensor", "dv", ev_o[o - 2])
                    for d in range(NCH):
                        p.op("tensor", lambda e, d=d, sl=sl: e.matmul(
                            psM[sl][:, 0:512], wst[:, sl * D + d * 128:sl * D + (d + 1) * 128],
                            xnv[d][:, 0:512], start=(d == 0), stop=False))
                        p.op("tensor", lambda e, d=d, sl=sl: e.matmul(
                            psM[sl][:, 512:1024], wst[:, sl * D + d * 128:sl * D + (d + 1) * 128],
                            xnv[d][:, 512:1024], start=(d == 0), stop=False))
                    if o == 0:
                        p.wait("tensor", "Dfsw", w2)
                    p.op("tensor", lambda e, o=o, sl=sl: e.matmul(
                        psM[sl][:, 0:512], b2s[:, o * 128:(o + 1) * 128], hw[:, 0:512],
                        start=False, stop=True))
                    mm_o[o] = p.op("tensor", lambda e, o=o, sl=sl: e.matmul(
                        psM[sl][:, 512:1024], b2s[:, o * 128:(o + 1) * 128], hw[:, 512:1024],
                        start=False, stop=True), "pe")
                    p.wait("scalar", "pe", mm_o[o])
                    p.wait("scalar", "dv", hbbv)
                    evs = p.op("scalar", lambda e, o=o, sl=sl: e.activation(
                        t1b[:, sl * 1024:(sl + 1) * 1024], psM[sl][:, :],
                        AF.Identity, bias=hbb[:, o:o + 1], scale=0.5), "ac")
                    p.wait("vector", "ac", evs)
                    p.wait("vector", f"Dxr{sl}", xl)
                    ev_o[o] = p.op("vector", lambda e, sl=sl: e.tensor_add(
                        xr[:, sl * 1024:(sl + 1) * 1024],
                        t1b[:, sl * 1024:(sl + 1) * 1024],
                        xr[:, sl * 1024:(sl + 1) * 1024]), "dv")
                    p.wait("sync", "dv", ev_o[o])
                    st_o[o] = p.dma("sync", dst[o * 128:(o + 1) * 128, :],
                                    xr[:, sl * 1024:(sl + 1) * 1024], f"t1{sl}")
                    if xcast is not None:
                        xcv, gcb, pcb = xcast
                        cv = p.op("vector", lambda e, o=o, sl=sl: e.tensor_copy(
                            xcv[o], xr[:, sl * 1024:(sl + 1) * 1024]), "dv")
                        cast_dv[o] = cv
                        pcb(o, cv)
                        if o % 8 == 7:
                            gcb(o // 8, cv)
                return st_o, cast_dv

    def alloc_stats(stk):
        PE = stk.enter_context
        return dict(
            sq=PE(sbuf("st_sq", [128, 1024], BF16)),
            stm=PE(sbuf("st_stm", [1, 1024], F32)),
            str_=PE(sbuf("st_str", [1, 1024], F32)),
            stv=PE(sbuf("st_stv", [1, 1024], F32)),
            stt=PE(sbuf("st_stt", [1, 1024], F32)),
            srow=PE(sbuf("st_srow", [1, 2048], BF16)),
            psA=PE(psum("st_psA", [128, 1024], F32)),
            psB=PE(psum("st_psB", [128, 1024], F32)),
        )

    def stats_phase(xnv, chunk_dep, b):
        sq, stm, str_ = b["sq"], b["stm"], b["str_"]
        stv, stt, srow = b["stv"], b["stt"], b["srow"]
        psA, psB = b["psA"], b["psB"]
        sqd = [None] * NCH
        mmd = [None] * NCH
        for c in range(NCH):
            if c >= 1:
                p.wait("scalar", "pe", mmd[c - 1])
            p.wait("scalar", "dv", chunk_dep(c))
            sqd[c] = p.op("scalar", lambda e, c=c: e.activation(
                sq[:, 0:1024], xnv[c], AF.Square), "ac")
            p.wait("tensor", "dv", chunk_dep(c))
            st, sp = (c == 0), (c == NCH - 1)
            p.op("tensor", lambda e, c=c, st=st, sp=sp: e.matmul(
                psA[0:1, 0:512], ones_c[:], xnv[c][:, 0:512], start=st, stop=sp))
            p.op("tensor", lambda e, c=c, st=st, sp=sp: e.matmul(
                psA[0:1, 512:1024], ones_c[:], xnv[c][:, 512:1024], start=st, stop=sp))
            p.wait("tensor", "ac", sqd[c])
            p.op("tensor", lambda e, st=st, sp=sp: e.matmul(
                psB[0:1, 0:512], ones_c[:], sq[:, 0:512], start=st, stop=sp))
            mmd[c] = p.op("tensor", lambda e, st=st, sp=sp: e.matmul(
                psB[0:1, 512:1024], ones_c[:], sq[:, 512:1024], start=st, stop=sp), "pe")
        p.wait("vector", "pe", mmd[NCH - 1])
        v = p.op("vector", lambda e: e.tensor_scalar_mul(stm[:], psA[0:1, :], 1.0 / D), "dv")
        p.wait("vector", "dv", v)
        v = p.op("vector", lambda e: e.tensor_mul(stt[:], stm[:], stm[:]), "dv")
        p.wait("vector", "dv", v)
        f0 = p.op("vector", lambda e: e.scalar_tensor_tensor(
            stv[:], psB[0:1, :], 1.0 / D, stt[:], ALU.mult, ALU.subtract), "dv")
        p.wait("scalar", "dv", f0)
        f1 = p.op("scalar", lambda e: e.activation(stt[:], stv[:], AF.Sqrt, bias=epsb[:]), "ac")
        p.wait("vector", "ac", f1)
        v = p.op("vector", lambda e: e.reciprocal(str_[:], stt[:]), "dv")
        p.wait("vector", "dv", v)
        v = p.op("vector", lambda e: e.tensor_mul(stv[:], stt[:], str_[:]), "dv")
        p.wait("vector", "dv", v)
        v = p.op("vector", lambda e: e.tensor_scalar(stv[:], stv[:], -1.0, 2.0, ALU.mult, ALU.add), "dv")
        p.wait("vector", "dv", v)
        v = p.op("vector", lambda e: e.tensor_mul(str_[:], str_[:], stv[:]), "dv")
        p.wait("vector", "dv", v)
        p.op("vector", lambda e: e.tensor_copy(srow[0:1, 0:1024], stm[:]))
        v = p.op("vector", lambda e: e.tensor_copy(srow[0:1, 1024:2048], str_[:]), "dv")
        p.wait("sync", "dv", v)
        sst = p.dma("sync", agsi, srow[:], "ags")
        p.wait("gpsimd", "Dags", sst)
        return p.op("gpsimd", lambda e: e.collective_compute(
            "AllGather", ALU.bypass, replica_groups=RG,
            ins=[agsi], outs=[agso]), "ccs")

    def tap_arena(name):
        if name not in tap_out:
            return None
        with contextlib.ExitStack() as ph:
            tb = ph.enter_context(sbuf("tapb", [128, 1024], F32))
            lastst = None
            last = None
            for c in range(NCH):
                if lastst is not None:
                    p.wait("vector", "Dtap", lastst)
                last = p.op("vector", lambda e, c=c: e.tensor_copy(tb[:], xn[c]), "dv")
                p.wait("sync", "dv", last)
                lastst = p.dma("sync", tap_out[name][c * 128:(c + 1) * 128, :], tb[:], "tap")
            return last

    xn1_dv, xn1_cdv = ln_phase(xT, xn, 0, [])
    t0dv = tap_arena("xn0")
    if t0dv is not None:
        xn1_dv = t0dv
    fdbg = nc.dram_tensor("tap_fdbg", [128, 3200], F32, kind="ExternalOutput").ap() if "fdbg" in taps else None
    ag1_cc = [None] * 4
    agst = [None] * 4

    def ag1_piece(o, dvv):
        if o % 4 != 3:
            return
        g, h = o // 8, (o // 4) % 2
        p.wait("scalar", "dv", dvv)
        agst[g] = p.dma("scalar", ag1i[g][:, h * 4096:(h + 1) * 4096],
                        xq[:, (g * 8 + h * 4) * 1024:(g * 8 + h * 4 + 4) * 1024],
                        f"ag1s{g}")

    def ag1_group(g, dvv):
        p.wait("gpsimd", f"Dag1s{g}", agst[g])
        ag1_cc[g] = p.op("gpsimd", lambda e, g=g: e.collective_compute(
            "AllGather", ALU.bypass, replica_groups=RG,
            ins=[ag1i[g]], outs=[ag1o[g]]), f"cc1_{g}")

    st2_sc = contextlib.ExitStack()
    st2_bufs = alloc_stats(st2_sc)
    ffn1_st, ffn1_cast = ffn_phase(xT, x1T, xn1_dv, xn, dbg=fdbg, xn_chunk_dv=xn1_cdv,
                                   xcast=(xnq, ag1_group, ag1_piece))
    if "x1" in tap_out:
        for s in ("Dt10", "Dt11"):
            p.wait("sync", s, p.now(s))
        for c in range(NCH):
            if c >= 1:
                p.wait("sync", "Dtap", p.now("Dtap"))
            p.dma("sync", tap_out["x1"][c * 128:(c + 1) * 128, :],
                  x1T[c * 128:(c + 1) * 128, :], "tap")

    ffn1_pe = p.now("pe")
    stats_cc = stats_phase(xnq, lambda c: ffn1_cast[c], st2_bufs)
    st2_sc.close()

    qkv_sc = contextlib.ExitStack()
    cos_s = qkv_sc.enter_context(nc.sbuf_tensor("cos_s", [64, SEQ], BF16))
    sin_s = qkv_sc.enter_context(nc.sbuf_tensor("sin_s", [64, SEQ], BF16))
    pshuf_s = qkv_sc.enter_context(nc.sbuf_tensor("pshuf_s", [64, 64], BF16))
    sld = [qkv_sc.enter_context(sbuf(f"q_sld{q}", [1, 2048], BF16)) for q in range(4)]
    scol = qkv_sc.enter_context(sbuf("q_scol", [128, 32], F32))
    rbb = qkv_sc.enter_context(sbuf("q_rbb", [128, 512], F32))
    p.dma("sync", cos_s[:], cosi, "cos")
    p.dma("sync", sin_s[:], sini, "cos")
    cr = p.dma("sync", pshuf_s[:], pshi, "cos")
    p.wait("tensor", "Dcos", cr)
    p.wait("vector", "Dcos", cr)
    p.wait("sync", "ccs", stats_cc)
    for q in range(4):
        p.dma("sync", sld[q][:], agso[q:q + 1, :], "ssl")
    p.wait("gpsimd", "ccs", stats_cc)
    for q in range(4):
        for k in range(8):
            ssl = p.dma("gpsimd", scol[:, q * 8 + k:q * 8 + k + 1],
                        agso[q:q + 1, 1024 + k * 128:1024 + (k + 1) * 128].rearrange(
                            "a (p b) -> p (a b)", p=128), "ssl")

    def qkv_pass(wsrc, kind, hls, arena_free_pe, prev_dv):
        with contextlib.ExitStack() as ph:
            PEr = ph.enter_context
            nw1p = PEr(sbuf("q_nw1p", [1, 1024], BF16))
            psW1 = PEr(psum("q_psW1", [1, 1024], F32))
            if kind == "qk":
                pc = PEr(sbuf("q_pc", [128, 4 * 512], BF16))
                ro1 = PEr(sbuf("q_ro1", [64, 512], BF16))
                ro2 = PEr(sbuf("q_ro2", [64, 512], BF16))
                psQ = [PEr(psum(f"q_psQ{i}", [128, 512], F32)) for i in range(2)]
                psSh = PEr(psum("q_psSh", [64, 512], F32))
                psRB = PEr(psum("q_psRB", [128, 512], F32))
            else:
                vp = PEr(sbuf("q_vp", [128, 2 * 1024], BF16))
                psV = [PEr(psum(f"q_psV{i}", [128, 1024], F32)) for i in range(2)]

            first_load = [True, True]
            p.wait("gpsimd", "pe", arena_free_pe)
            wld = [p.dma("gpsimd", arena[:, g * 8192:(g + 1) * 8192],
                         wsrc[:, g * 8192:(g + 1) * 8192], f"war{g}") for g in range(4)]
            p.wait("tensor", "dv", prev_dv)
            for d in range(NCH):
                if d % 8 == 0:
                    p.wait("tensor", f"Dwar{d // 8}", wld[d // 8])
                st_, sp_ = (d == 0), (d == NCH - 1)
                p.op("tensor", lambda e, d=d, st_=st_, sp_=sp_: e.matmul(
                    psW1[0:1, 0:512], ones_c[:], arena[:, d * 1024:d * 1024 + 512],
                    start=st_, stop=sp_))
                w1mm = p.op("tensor", lambda e, d=d, st_=st_, sp_=sp_: e.matmul(
                    psW1[0:1, 512:1024], ones_c[:], arena[:, d * 1024 + 512:(d + 1) * 1024],
                    start=st_, stop=sp_), "pe" if sp_ else None)
            p.wait("vector", "pe", w1mm)
            nw1_dv = p.op("vector", lambda e: e.tensor_scalar_mul(
                nw1p[:], psW1[0:1, :], -1.0), "dv")
            p.wait("tensor", "dv", nw1_dv)
            p.wait("tensor", "Dssl", ssl)
            xq_t = [None, None]
            last_mm = [0, 0]
            pc_store = {}
            pc_copy = {}
            rope_dv = None
            rbb_dv = None
            first_tensor = True
            for tc in range(8):
                sl = tc % 2
                blk, half = tc // 2, tc % 2
                if xq_t[sl] is not None:
                    p.wait("sync", "pe", last_mm[sl])
                elif first_load[sl]:
                    for gg in range(4):
                        p.wait("sync", f"Dag1s{gg}", agst[gg])
                    first_load[sl] = False
                xqg = [None] * 4
                for g in range(4):
                    p.wait("sync", f"cc1_{g}", ag1_cc[g])
                    src_g = ag1o[g][blk * 128:(blk + 1) * 128, :].rearrange(
                        "p (c t) -> p c t", c=8)[:, :, half * 512:(half + 1) * 512]
                    xqg[g] = p.dma(
                        "sync",
                        xq[:, sl * NCH * 512 + g * 4096:sl * NCH * 512 + (g + 1) * 4096].rearrange(
                            "p (c t) -> p c t", c=8), src_g, f"xq{sl}{g}")
                xq_t[sl] = xqg
                if kind == "qk":
                    if rbb_dv is not None:
                        p.wait("tensor", "dv", rbb_dv)
                    rbm = p.op("tensor", lambda e, blk=blk, half=half: e.matmul(
                        psRB[:, :], ones_r[:],
                        sld[blk][0:1, 1024 + half * 512:1024 + (half + 1) * 512],
                        start=True, stop=True), "pe")
                    p.wait("vector", "pe", rbm)
                    rbb_dv = p.op("vector", lambda e: e.tensor_copy(rbb[:], psRB[:, :]), "dv")
                    for j in range(8):
                        rot, rot4 = j % 2, j % 4
                        if pc_copy.get(rot) is not None:
                            p.wait("tensor", "dv", pc_copy[rot])
                        for d in range(NCH):
                            if j == 0 and d % 8 == 0:
                                p.wait("tensor", f"Dxq{sl}{d // 8}", xq_t[sl][d // 8])
                            p.op("tensor", lambda e, j=j, d=d, sl=sl, rot=rot: e.matmul(
                                psQ[rot][:, :],
                                arena[:, d * 1024 + j * 128:d * 1024 + (j + 1) * 128],
                                xq[:, sl * NCH * 512 + d * 512:sl * NCH * 512 + (d + 1) * 512],
                                start=(d == 0), stop=False))
                        mmq = p.op("tensor", lambda e, j=j, blk=blk, half=half, rot=rot: e.matmul(
                            psQ[rot][:, :], nw1p[0:1, j * 128:(j + 1) * 128],
                            sld[blk][0:1, half * 512:(half + 1) * 512],
                            start=False, stop=True), "pe")
                        last_mm[sl] = mmq
                        p.wait("vector", "pe", mmq)
                        if pc_store.get(rot4) is not None:
                            p.wait("vector", f"Dpst{rot4}", pc_store[rot4])
                        cpq = p.op("vector", lambda e, rot=rot, rot4=rot4: e.tensor_mul(
                            pc[:, rot4 * 512:(rot4 + 1) * 512], psQ[rot][:, :], rbb[:]), "dv")
                        pc_copy[rot] = cpq
                        is_q = (j < 4)
                        hl = hls[(j // 2) % 2]
                        fc = j % 2
                        dstT = qTd if is_q else kTd
                        col = (hl * 2 + fc) * SEQ + tc * 512
                        if fc == 0:
                            p.wait("tensor", "dv", cpq)
                            if rope_dv is not None:
                                p.wait("tensor", "dv", rope_dv)
                            shm = p.op("tensor", lambda e, rot4=rot4: e.matmul(
                                psSh[:, :], pshuf_s[:, :],
                                pc[0:64, rot4 * 512:(rot4 + 1) * 512],
                                start=True, stop=True), "pe")
                            p.wait("vector", "pe", shm)
                            p.op("vector", lambda e, rot4=rot4, tc=tc: e.tensor_mul(
                                ro1[:], pc[0:64, rot4 * 512:(rot4 + 1) * 512],
                                cos_s[:, tc * 512:(tc + 1) * 512]))
                            p.op("vector", lambda e, tc=tc: e.tensor_mul(
                                ro2[:], psSh[0:64, :], sin_s[:, tc * 512:(tc + 1) * 512]))
                            rope_dv = p.op("vector", lambda e, rot4=rot4: e.tensor_add(
                                pc[0:64, rot4 * 512:(rot4 + 1) * 512], ro1[:], ro2[:]), "dv")
                            p.wait("sync", "dv", rope_dv)
                        else:
                            p.wait("sync", "dv", cpq)
                        pc_store[rot4] = p.dma("sync", dstT[:, col:col + 512],
                                               pc[:, rot4 * 512:(rot4 + 1) * 512], f"pst{rot4}")
                else:
                    for sub in range(4):
                        rot = sub % 2
                        if pc_copy.get(("v", rot)) is not None:
                            p.wait("tensor", "dv", pc_copy[("v", rot)])
                        for d in range(NCH):
                            if sub == 0 and d % 8 == 0:
                                p.wait("tensor", f"Dxq{sl}{d // 8}", xq_t[sl][d // 8])
                            p.op("tensor", lambda e, d=d, sl=sl, sub=sub, rot=rot: e.matmul(
                                psV[rot][:, 0:512],
                                xq[:, sl * NCH * 512 + d * 512 + sub * 128:sl * NCH * 512 + d * 512 + (sub + 1) * 128],
                                arena[:, d * 1024:d * 1024 + 512],
                                start=(d == 0), stop=False))
                            p.op("tensor", lambda e, d=d, sl=sl, sub=sub, rot=rot: e.matmul(
                                psV[rot][:, 512:1024],
                                xq[:, sl * NCH * 512 + d * 512 + sub * 128:sl * NCH * 512 + d * 512 + (sub + 1) * 128],
                                arena[:, d * 1024 + 512:(d + 1) * 1024],
                                start=(d == 0), stop=False))
                        p.op("tensor", lambda e, blk=blk, half=half, sub=sub, rot=rot: e.matmul(
                            psV[rot][:, 0:512],
                            sld[blk][0:1, half * 512 + sub * 128:half * 512 + (sub + 1) * 128],
                            nw1p[0:1, 0:512], start=False, stop=True))
                        mmv = p.op("tensor", lambda e, blk=blk, half=half, sub=sub, rot=rot: e.matmul(
                            psV[rot][:, 512:1024],
                            sld[blk][0:1, half * 512 + sub * 128:half * 512 + (sub + 1) * 128],
                            nw1p[0:1, 512:1024], start=False, stop=True), "pe")
                        last_mm[sl] = mmv
                        p.wait("vector", "pe", mmv)
                        if pc_store.get(("v", rot)) is not None:
                            p.wait("vector", f"Dvst{rot}", pc_store[("v", rot)])
                        rix = blk * 8 + half * 4 + sub
                        cpv = p.op("vector", lambda e, rot=rot, rix=rix: e.tensor_scalar(
                            vp[:, rot * 1024:(rot + 1) * 1024], psV[rot][:, :],
                            scol[:, rix:rix + 1], None, ALU.mult), "dv")
                        pc_copy[("v", rot)] = cpv
                        p.wait("sync", "dv", cpv)
                        pc_store[("v", rot)] = p.dma("sync", vTd[:, (tc * 4 + sub) * 1024:(tc * 4 + sub + 1) * 1024],
                                                     vp[:, rot * 1024:(rot + 1) * 1024], f"vst{rot}")
            return p.now("pe"), p.now("dv")

    qk1_pe, qk1_dv = qkv_pass(wqkA, "qk", (0, 1), ffn1_pe, ffn1_cast[NCH - 1])
    qk2_pe, qk2_dv = qkv_pass(wqkB, "qk", (2, 3), qk1_pe, qk1_dv)
    _, qkv_dv = qkv_pass(wvTi, "v", (), qk2_pe, qk2_dv)
    qkv_sc.close()

    ag2_cc = [[None] * 4 for _ in range(NHL)]
    at_start_pe = p.now("pe")
    sel_hist = []
    sel_sync_init = [False]
    with contextlib.ExitStack() as ph:
        PEr = ph.enter_context
        cm_s = PEr(sbuf("cm_s", [128, 4 * 512], BF16))
        PT_D = 6
        pt = PEr(sbuf("at_pt", [128, PT_D * 512], BF16))
        ssum = PEr(sbuf("at_ss", [1, 512], F32))
        rin = PEr(sbuf("at_ri", [1, 512], F32))
        rbq = PEr(sbuf("at_rb", [128, 2 * 512], F32))
        ao = PEr(sbuf("at_ao", [128, 4 * 512], BF16))
        tg = PEr(sbuf("op_tg", [128, 2 * 1024], BF16))
        psS = [PEr(psum(f"at_psS{i}", [128, 512], F32)) for i in range(2)]
        psP = [PEr(psum(f"at_psP{i}", [128, 1024], F32)) for i in range(2)]
        psU = PEr(psum("at_psU", [1, 512], F32))
        psR2 = PEr(psum("at_psR2", [128, 512], F32))
        for s in ("Dpst0", "Dpst1", "Dpst2", "Dpst3", "Dvst0", "Dvst1"):
            p.wait("sync", s, p.now(s))
        cl = p.dma("sync", cm_s[:], cmi, "cmk")

        qh = xq[:, 0:2 * SEQ]
        kh = xq[:, 2 * SEQ:4 * SEQ]
        vh3 = xq[:, 4 * SEQ:4 * SEQ + NCH * 256].rearrange("p (c f) -> p c f", c=NCH)

        sc_pe = {}
        exp_ac = {}
        gate = {}
        pv_pe = {}
        sum_pe = {}

        def emit_ag2(hl, half):
            for s in ("Dao00", "Dao01", "Dao10", "Dao11"):
                p.wait("gpsimd", s, p.now(s))
            ag2_cc[hl][half] = p.op("gpsimd", lambda e, hl=hl, half=half: e.collective_compute(
                "AllGather", ALU.bypass, replica_groups=RG,
                ins=[ag2i[hl][half]], outs=[ag2o[hl][half]]), f"c2{hl}{half}")

        sel_cc_waited = set()

        def emit_select_item(hl, half, gch, qp):
            h, piece = gch // 2, gch % 2
            r = h // 4
            if not sel_sync_init[0]:
                p.wait("gpsimd", "pe", at_start_pe)
                sel_sync_init[0] = True
            if (hl, half) not in sel_cc_waited:
                p.wait("gpsimd", f"c2{hl}{half}", ag2_cc[hl][half])
                sel_cc_waited.add((hl, half))
            if True:
                if True:
                    it = len(sel_hist)
                    sl = it % 2
                    if it >= 2:
                        p.wait("gpsimd", "dv", sel_hist[it - 2])
                    tgl = p.dma("gpsimd", tg[:, sl * 1024:(sl + 1) * 1024],
                                ag2o[hl][half][r * 256 + piece * 128:r * 256 + (piece + 1) * 128,
                                               :], f"tg{sl}")
                    p.wait("vector", f"Dtg{sl}", tgl)
                    if qp == 0:
                        sd = p.op("vector", lambda e, gch=gch, sl=sl, qp=qp: e.tensor_scalar(
                            arena[:, gch * 1024:(gch + 1) * 1024],
                            tg[:, sl * 1024:(sl + 1) * 1024], m4[:, qp:qp + 1], None, ALU.mult), "dv")
                    else:
                        sd = p.op("vector", lambda e, gch=gch, sl=sl, qp=qp: e.scalar_tensor_tensor(
                            arena[:, gch * 1024:(gch + 1) * 1024],
                            tg[:, sl * 1024:(sl + 1) * 1024], m4[:, qp:qp + 1],
                            arena[:, gch * 1024:(gch + 1) * 1024], ALU.mult, ALU.add), "dv")
                    sel_hist.append(sd)

        hq_v = {}
        hk_v = {}
        hv_v = {}

        def sel_items(hl, qps=(0, 1, 2, 3)):
            return [(hl, qp, gch, qp)
                    for qp in qps
                    for gch in range(NCH) if (gch // 2) % 4 == hl]

        sel_end2 = [None]
        for hl in range(NHL):
            p.wait("sync", "pe", p.now("pe"))
            sel_todo = sel_items(hl - 1) if hl >= 1 else []
            sel3a = sel_items(hl, (0, 1, 2)) if hl == NHL - 1 else []

            def load_chunk(c, hl=hl):
                sl4 = c % 4
                for kindname, vals in (("hq", hq_v), ("hk", hk_v), ("hv", hv_v)):
                    if (c - 4) in vals:
                        p.wait("sync", f"D{kindname}{sl4}", vals[c - 4])
                dstT = {"hq": qTd, "hk": kTd}
                for kindname, vals, buf in (("hq", hq_v, qh), ("hk", hk_v, kh)):
                    for fc in range(2):
                        v = p.dma("sync", buf[:, fc * SEQ + c * 512:fc * SEQ + (c + 1) * 512],
                                  dstT[kindname][:, (hl * 2 + fc) * SEQ + c * 512:
                                                 (hl * 2 + fc) * SEQ + (c + 1) * 512],
                                  f"D{kindname}{sl4}"[1:])
                    vals[c] = v
                hv_v[c] = p.dma(
                    "sync", vh3[:, c * 4:(c + 1) * 4, :],
                    vTd.rearrange("p (c f) -> p c f", c=NCH)[:, c * 4:(c + 1) * 4,
                                                            hl * 256:(hl + 1) * 256],
                    f"hv{sl4}")

            load_chunk(0)
            load_chunk(1)

            tiles = [(qc, kt) for qc in range(8) for kt in range(4 * qc + 4)]
            T = len(tiles)
            first_i = {qc: sum(4 * q + 4 for q in range(qc)) for qc in range(8)}
            last_i = {qc: first_i[qc] + 4 * qc + 3 for qc in range(8)}
            G = hl * T

            sscopy_dv = {}
            rd_dv = {}
            rbm_pe = {}
            aodv_dv = {}
            ao_store = {}

            seen_chunks = set()

            def emit_sc(i):
                qc, kt = tiles[i]
                gi = G + i
                rot = gi % 2
                if i == 0:
                    p.wait("tensor", "Dcmk", cl)
                    if hl == 0:
                        p.wait("tensor", "dv", qkv_dv)
                kc = kt // 4
                if kc not in seen_chunks or qc not in seen_chunks:
                    for c in {kc, qc}:
                        p.wait("tensor", f"Dhq{c % 4}", hq_v[c])
                        p.wait("tensor", f"Dhk{c % 4}", hk_v[c])
                        p.wait("tensor", f"Dhv{c % 4}", hv_v[c])
                        seen_chunks.add(c)
                if gi >= 2 and (gi - 2) in exp_ac:
                    p.wait("tensor", "ac", exp_ac[gi - 2])
                p.op("tensor", lambda e, kt=kt, qc=qc, rot=rot: e.matmul(
                    psS[rot][:, :], kh[:, kt * 128:(kt + 1) * 128],
                    qh[:, qc * 512:(qc + 1) * 512], start=True, stop=False))
                smm = p.op("tensor", lambda e, kt=kt, qc=qc, rot=rot: e.matmul(
                    psS[rot][:, :], kh[:, SEQ + kt * 128:SEQ + (kt + 1) * 128],
                    qh[:, SEQ + qc * 512:SEQ + (qc + 1) * 512], start=False, stop=True), "pe")
                sc_pe[gi] = smm
                p.wait("scalar", "pe", smm)
                if (gi - PT_D) in sum_pe:
                    p.wait("scalar", "pe", sum_pe[gi - PT_D])
                pslot = gi % PT_D
                ex = p.op("scalar", lambda e, rot=rot, pslot=pslot: e.activation(
                    pt[:, pslot * 512:(pslot + 1) * 512], psS[rot][:, :],
                    AF.Exp, scale=SC), "ac")
                exp_ac[gi] = ex
                if kt >= 4 * qc:
                    jj = kt - 4 * qc
                    p.wait("vector", "ac", ex)
                    mk = p.op("vector", lambda e, jj=jj, pslot=pslot: e.tensor_mul(
                        pt[:, pslot * 512:(pslot + 1) * 512],
                        pt[:, pslot * 512:(pslot + 1) * 512],
                        cm_s[:, jj * 512:(jj + 1) * 512]), "dv")
                    gate[gi] = ("dv", mk)
                else:
                    gate[gi] = ("ac", ex)

            def emit_pv(i):
                qc, kt = tiles[i]
                gi = G + i
                par = qc % 2
                pslot = gi % PT_D
                st, sp = (kt == 0), (kt == 4 * qc + 3)
                gs, gv = gate[gi]
                p.wait("tensor", gs, gv)
                if st and (qc - 2) in aodv_dv:
                    p.wait("tensor", "dv", aodv_dv[qc - 2])
                p.op("tensor", lambda e, kt=kt, par=par, pslot=pslot, st=st, sp=sp: e.matmul(
                    psP[par][:, 0:512], vh3[:, kt, 0:128],
                    pt[:, pslot * 512:(pslot + 1) * 512], start=st, stop=sp))
                pv_pe[gi] = p.op("tensor", lambda e, kt=kt, par=par, pslot=pslot, st=st, sp=sp: e.matmul(
                    psP[par][:, 512:1024], vh3[:, kt, 128:256],
                    pt[:, pslot * 512:(pslot + 1) * 512], start=st, stop=sp), "pe")

            def emit_sum(i):
                qc, kt = tiles[i]
                gi = G + i
                pslot = gi % PT_D
                st, sp = (kt == 0), (kt == 4 * qc + 3)
                if st and (qc - 1) in sscopy_dv:
                    p.wait("tensor", "dv", sscopy_dv[qc - 1])
                sum_pe[gi] = p.op("tensor", lambda e, pslot=pslot, st=st, sp=sp: e.matmul(
                    psU[0:1, :], ones_c[:], pt[:, pslot * 512:(pslot + 1) * 512],
                    start=st, stop=sp), "pe")
                if sp:
                    p.wait("vector", "pe", pv_pe[G + last_i[qc]])
                    p.wait("vector", "pe", sum_pe[gi])
                    v2 = p.op("vector", lambda e: e.tensor_copy(ssum[:], psU[0:1, :]), "dv")
                    sscopy_dv[qc] = v2
                    p.wait("vector", "dv", v2)
                    rd_dv[qc] = p.op("vector", lambda e: e.reciprocal(rin[:], ssum[:]), "dv")

            def emit_evac_tail(qc):
                par = qc % 2
                p.wait("tensor", "dv", rd_dv[qc])
                rbm_pe[qc] = p.op("tensor", lambda e: e.matmul(
                    psR2[:, :], ones_rf[:], rin[0:1, :], start=True, stop=True), "pe")
                p.wait("vector", "pe", rbm_pe[qc])
                for half in range(2):
                    key = (par, half)
                    if key in ao_store:
                        p.wait("vector", f"Dao{par}{half}", ao_store[key])
                p.op("vector", lambda e, par=par: e.tensor_copy(
                    rbq[:, par * 512:(par + 1) * 512], psR2[:, :]))
                p.op("vector", lambda e, par=par: e.tensor_mul(
                    ao[:, par * 1024:par * 1024 + 512], psP[par][:, 0:512],
                    rbq[:, par * 512:(par + 1) * 512]))
                aodv_dv[qc] = p.op("vector", lambda e, par=par: e.tensor_mul(
                    ao[:, par * 1024 + 512:(par + 1) * 1024], psP[par][:, 512:1024],
                    rbq[:, par * 512:(par + 1) * 512]), "dv")
                p.wait("sync", "dv", aodv_dv[qc])
                hf, off = qc // 2, (qc % 2) * 512
                ao_store[(par, 0)] = p.dma(
                    "sync", ag2i[hl][hf][0:128, off:off + 512],
                    ao[:, par * 1024:par * 1024 + 512], f"ao{par}0")
                ao_store[(par, 1)] = p.dma(
                    "sync", ag2i[hl][hf][128:256, off:off + 512],
                    ao[:, par * 1024 + 512:(par + 1) * 1024], f"ao{par}1")

            SUM_LAG = 2
            LOOK = 3
            i_sc = 0
            pending = []
            loaded = {0, 1}
            for i in range(T):
                if sel_todo and i % 4 == 1:
                    emit_select_item(*sel_todo.pop(0))
                    if not sel_todo and hl == NHL - 1:
                        sel_end2[0] = sel_hist[-1]
                qc_now = tiles[i][0]
                if qc_now + 2 <= 7 and (qc_now + 2) not in loaded:
                    load_chunk(qc_now + 2)
                    loaded.add(qc_now + 2)
                while i_sc < min(i + 1 + LOOK, T):
                    emit_sc(i_sc)
                    i_sc += 1
                emit_pv(i)
                if i - SUM_LAG >= 0:
                    emit_sum(i - SUM_LAG)
                for qc in list(pending):
                    if i >= last_i[qc] + 6:
                        emit_evac_tail(qc)
                        pending.remove(qc)
                        if qc in (1, 3, 5):
                            emit_ag2(hl, qc // 2)
                qc, kt = tiles[i]
                if i - SUM_LAG >= 0 and tiles[i - SUM_LAG][1] == 4 * tiles[i - SUM_LAG][0] + 3:
                    pending.append(tiles[i - SUM_LAG][0])
            for i in range(T - SUM_LAG, T):
                emit_sum(i)
                if tiles[i][1] == 4 * tiles[i][0] + 3:
                    pending.append(tiles[i][0])
            for qc in pending:
                emit_evac_tail(qc)
            emit_ag2(hl, 3)
            for itm in sel_todo:
                emit_select_item(*itm)
            if sel_todo and hl == NHL - 1:
                sel_end2[0] = sel_hist[-1]
            for itm in sel3a:
                emit_select_item(*itm)
        opw = [p.dma("gpsimd", wst_p[:, j * D:(j + 1) * D],
                     woT[j * 128:(j + 1) * 128, :], f"wst{j}") for j in range(2)]
        for itm in sel_items(NHL - 1, (3,)):
            emit_select_item(*itm)
        aT_dv = sel_hist[-1]
        sel_end2 = sel_end2[0]
        attn_evac_dv = aodv_dv[7]
        attn_ac = p.now("ac")

    attn_pe = p.now("pe")
    ln3_sc = contextlib.ExitStack()
    ln3_bufs = alloc_ln(ln3_sc)
    with contextlib.ExitStack() as ph:
        PEr = ph.enter_context
        wst = wst_p
        xr = PEr(sbuf("op_xr", [128, 2 * 1024], F32))
        psM = [PEr(psum(f"op_psM{i}", [128, 1024], F32)) for i in range(2)]
        p.wait("sync", "dv", aT_dv)
        mm_o = [None] * NCH
        ev_o = [None] * NCH
        st_o = [None] * NCH
        for j in range(NCH):
            sl = j % 2
            if j < 2:
                wl = opw[j]
            else:
                p.wait("gpsimd", "pe", mm_o[j - 2])
                wl = p.dma("gpsimd", wst[:, sl * D:(sl + 1) * D],
                           woT[j * 128:(j + 1) * 128, :], f"wst{sl}")
            if j >= 2:
                p.wait("sync", "dv", ev_o[j - 2])
                p.wait("sync", f"Dt1{sl}", st_o[j - 2])
            xl = p.dma("sync", xr[:, sl * 1024:(sl + 1) * 1024],
                       x1T[j * 128:(j + 1) * 128, :], f"xr{sl}")
            p.wait("tensor", f"Dwst{sl}", wl)
            p.wait("tensor", "dv", sel_end2)
            if j == 0:
                p.wait("tensor", "dv", attn_evac_dv)
                p.wait("tensor", "ac", attn_ac)
            if j >= 2:
                p.wait("tensor", "dv", ev_o[j - 2])
            mm = None
            c_order = ([c for c in range(NCH) if (c // 2) % 4 != 3]
                       + [c for c in range(NCH) if (c // 2) % 4 == 3])
            for idx, c in enumerate(c_order):
                if idx == NCH - 8:
                    p.wait("tensor", "dv", aT_dv)
                p.op("tensor", lambda e, c=c, sl=sl, idx=idx: e.matmul(
                    psM[sl][:, 0:512], wst[:, sl * D + c * 128:sl * D + (c + 1) * 128],
                    arena[:, c * 1024:c * 1024 + 512], start=(idx == 0), stop=(idx == NCH - 1)))
                mm = p.op("tensor", lambda e, c=c, sl=sl, idx=idx: e.matmul(
                    psM[sl][:, 512:1024], wst[:, sl * D + c * 128:sl * D + (c + 1) * 128],
                    arena[:, c * 1024 + 512:(c + 1) * 1024], start=(idx == 0), stop=(idx == NCH - 1)),
                    "pe" if idx == NCH - 1 else None)
            mm_o[j] = mm
            p.wait("vector", "pe", mm_o[j])
            p.wait("vector", f"Dxr{sl}", xl)
            ev_o[j] = p.op("vector", lambda e, sl=sl: e.tensor_add(
                xr[:, sl * 1024:(sl + 1) * 1024], psM[sl][:, :],
                xr[:, sl * 1024:(sl + 1) * 1024]), "dv")
            p.wait("sync", "dv", ev_o[j])
            st_o[j] = p.dma("sync", x2T[j * 128:(j + 1) * 128, :],
                            xr[:, sl * 1024:(sl + 1) * 1024], f"t1{sl}")
        op_st = st_o

    if "x2" in tap_out:
        for s in ("Dt10", "Dt11"):
            p.wait("sync", s, p.now(s))
        for c in range(NCH):
            if c >= 1:
                p.wait("sync", "Dtap", p.now("Dtap"))
            p.dma("sync", tap_out["x2"][c * 128:(c + 1) * 128, :],
                  x2T[c * 128:(c + 1) * 128, :], "tap")

    xn3_dv, xn3_cdv = ln_phase(x2T, xnq, attn_pe, [], bufs=ln3_bufs,
                      src_ready=lambda c: (f"Dt1{c % 2}", op_st[c]))
    ln3_sc.close()
    ffn_phase(x2T, yT, xn3_dv, xnq, src_slots=("Dt10", "Dt11"), xn_chunk_dv=xn3_cdv,
              wst_free_pe=p.now("pe"))[0]

    p.wait("sync", "Dt10", p.now("Dt10"))
    p.wait("sync", "Dt11", p.now("Dt11"))
    if taps:
        p.wait("sync", "Dtap", p.now("Dtap"))

    p.emit(nc)
    es.close()
    return nc


def _consts():
    inv = 10000.0 ** (-np.arange(0, 32, dtype=np.float64) * 2.0 / 64.0)
    pos = np.arange(SEQ, dtype=np.float64)
    ang = pos[None, :] * inv[:, None]
    cos2 = np.repeat(np.cos(ang), 2, axis=0)
    sin2 = np.repeat(np.sin(ang), 2, axis=0)
    psh = np.zeros((64, 64), np.float32)
    for i in range(32):
        psh[2 * i + 1, 2 * i] = -1.0
        psh[2 * i, 2 * i + 1] = 1.0
    selm = np.zeros((8, 128), np.float32)
    for e_ in range(8):
        selm[e_, e_ * 16:(e_ + 1) * 16] = 2.0
    cm = np.zeros((128, 4 * 512), np.float32)
    kt = np.arange(128)[:, None]
    qt = np.arange(512)[None, :]
    for j in range(4):
        cm[:, j * 512:(j + 1) * 512] = (qt >= (j * 128 + kt)).astype(np.float32)
    return {
        "cosi": cos2.astype(bf16),
        "sini": sin2.astype(bf16),
        "pshi": psh.astype(bf16),
        "seli": selm.astype(bf16),
        "cmi": cm.astype(bf16),
        "idi": np.eye(128, dtype=np.float32).astype(bf16),
        "ioi": np.tile(np.arange(8, dtype=np.float32), (16, 1)),
    }


def _blk(w):
    return np.ascontiguousarray(
        w.reshape(32, 128, 32, 128).transpose(0, 3, 2, 1).reshape(4096, 4096))


def _wrows(m):
    return np.ascontiguousarray(
        m.reshape(1024, 32, 128).transpose(2, 1, 0).reshape(128, 32 * 1024))


def host_inputs(L0_attn_hidden_states, L0_attn_w_q, L0_attn_w_k, L0_attn_w_v,
                L0_attn_w_out, L0_ffn_base_W, L0_ffn_base_b, L0_ffn_router_W,
                L0_ffn_router_b, L0_ffn_A, L0_ffn_B, L0_ffn_expert_map):
    x = np.asarray(L0_attn_hidden_states, np.float32)
    wq = np.asarray(L0_attn_w_q, np.float32)
    wk = np.asarray(L0_attn_w_k, np.float32)
    wv = np.asarray(L0_attn_w_v, np.float32)
    wo = np.asarray(L0_attn_w_out, np.float32)
    bW = np.asarray(L0_ffn_base_W, np.float32)
    bb = np.asarray(L0_ffn_base_b, np.float32)
    rW = np.asarray(L0_ffn_router_W, np.float32)
    rb = np.asarray(L0_ffn_router_b, np.float32)
    A = np.asarray(L0_ffn_A, np.float32)
    B = np.asarray(L0_ffn_B, np.float32)
    em = np.asarray(L0_ffn_expert_map, np.int32)

    consts = _consts()
    shared = {
        "bWt": _blk(bW),
        "woT": _blk(wo),
        "a2i": np.ascontiguousarray(
            A.reshape(128, D).T.reshape(32, 128, 128).transpose(1, 0, 2).reshape(128, 32 * 128)),
        "b2i": np.ascontiguousarray(B.transpose(0, 2, 1).reshape(128, D)),
        "rwi": np.ascontiguousarray(
            rW.T.reshape(32, 128, 15).transpose(1, 0, 2).reshape(128, 32 * 15)),
        "bbi": np.ascontiguousarray(bb.reshape(32, 128).T),
        "rbi": np.ascontiguousarray(rb.reshape(1, 15)),
        "emi": np.concatenate([em, np.array([-1], np.int32)]).reshape(16, 1),
    }
    in_maps = []
    for c in range(8):
        b, q = c // 4, c % 4
        h0 = 4 * q
        xs = np.ascontiguousarray(x[b].T[:, q * TL:(q + 1) * TL])
        m4a = np.zeros((128, 4), np.float32)
        m4a[:, q] = 1.0
        wqk_a = np.concatenate([wq[h0 * 256:(h0 + 2) * 256],
                                wk[h0 * 256:(h0 + 2) * 256]], 0)
        wqk_b = np.concatenate([wq[(h0 + 2) * 256:(h0 + 4) * 256],
                                wk[(h0 + 2) * 256:(h0 + 4) * 256]], 0)
        wv_c = wv[h0 * 256:(h0 + 4) * 256]
        im = dict(shared)
        im.update(consts)
        im.update({
            "xT": xs,
            "m4i": m4a,
            "wqkA": _wrows(wqk_a),
            "wqkB": _wrows(wqk_b),
            "wvTi": _wrows(wv_c),
        })
        in_maps.append(im)
    return in_maps


def kernel(**inputs):
    in_maps = host_inputs(**inputs)
    nc = build_program()
    res = run_bass_kernel_spmd(nc, in_maps, core_ids=list(range(8)))
    out = np.empty((2, SEQ, D), np.float32)
    for c in range(8):
        b, q = c // 4, c % 4
        out[b, q * TL:(q + 1) * TL, :] = res.results[c]["yT"].T
    return out
